# version 02
# baseline (speedup 1.0000x reference)
import math
import numpy as np
import ml_dtypes
from contextlib import ExitStack

import concourse.bass as bass
import concourse.mybir as mybir
from concourse.bass_utils import run_bass_kernel_spmd

F32 = mybir.dt.float32
BF16 = mybir.dt.bfloat16
FP16 = mybir.dt.float16
AF = mybir.ActivationFunctionType
ALU = mybir.AluOpType
AX = mybir.AxisListType

D = 4096
T = 8192
NC = 8
NT = 8
TT = 128
TOWN = NT * TT
H = 32
HD = 128
NDC = 32
EPS = 1e-5
QSCALE = 1.0 / math.sqrt(HD)
RANKS = (8, 16, 32, 64)
NEGBIG = -30000.0
P1_DBG = 0
P3_LOOKAHEAD = 1


class SC:
    def __init__(self, ctx, nc, name):
        self.h = ctx.enter_context(nc.semaphore(name))
        self.v = 0

    def inc(self, n=1):
        self.v += n
        return self.h, n


def _p3_units(n_heads):
    return [(h, half) for h in range(n_heads) for half in range(2)]


def _nkb(half):
    return 32 if half == 0 else 64


def _off(half, kb):
    js = (kb // 8) if half == 0 else max(0, kb // 8 - 4)
    return js * 128


def _masked_octet(half, o):
    return True if half == 0 else (o >= 4)


def _p3_pe_events(n_heads):
    units = _p3_units(n_heads)
    NU = len(units)
    ev = []
    for kb in range(_nkb(units[0][1])):
        ev.append(("kb", 0, kb))
    for u in range(NU):
        if u + 1 < NU:
            for kb in range(4):
                ev.append(("kb", u + 1, kb))
        ev.append(("T", u))
        ev.append(("mb", u))
        if u + 1 < NU:
            for kb in range(4, _nkb(units[u + 1][1])):
                ev.append(("kb", u + 1, kb))
        ev.append(("AV", u))
        ev.append(("rbc", u))
    return ev


def _p3_marks(n_heads, ap0, ae0, at0, st0):
    m = {}
    units = _p3_units(n_heads)
    NU = len(units)
    ap = ap0
    for e in _p3_pe_events(n_heads):
        ap += 1
        if e[0] == "kb":
            m[("kbv", e[1], e[2])] = ap
        elif e[0] == "T":
            m[("tposed", e[1])] = ap
        elif e[0] == "mb":
            m[("mbmm", e[1])] = ap
        elif e[0] == "AV":
            m[("avg", e[1])] = ap
        elif e[0] == "rbc":
            m[("rbcmm", e[1])] = ap
    m["ap_end"] = ap
    ae = ae0
    evacv = []
    for u in range(NU):
        _, half = units[u]
        m[("evbase", u)] = ae
        for _kb in range(_nkb(half)):
            ae += 1
            evacv.append(ae)
        m[("evac", u)] = ae
        ae += 1; m[("mbc", u)] = ae
        ae += 1; m[("exp", u)] = ae
    m["ae_end"] = ae
    m["evacv"] = evacv
    at = at0
    for u in range(NU):
        at += 1; m[("oct", u)] = at
        at += 1; m[("max", u)] = at
        at += 1; m[("redge", u)] = at
        at += 1; m[("ediag", u)] = at
        at += 1; m[("sub", u)] = at
        at += 1; m[("recip", u)] = at
        at += 1; m[("rbccp", u)] = at
        at += 1; m[("avscaled", u)] = at
    m["at_end"] = at
    st = st0
    for u in range(NU):
        st += 16; m[("avstored", u)] = st
    m["st_end"] = st
    return m


def build_program(n_heads=H, phase_limit=99, taps=()):
    nc = bass.Bass("TRN2", target_bir_lowering=False)

    x_sl = nc.dram_tensor("x_sl", [TOWN, D], F32, kind="ExternalInput").ap()
    w_sl = {wn: nc.dram_tensor(f"{wn}_sl", [D // NC, D], F32, kind="ExternalInput").ap()
            for wn in ("wq", "wk", "wv", "wo", "wf")}
    b_in = {bn: nc.dram_tensor(f"{bn}_in", [1, D], F32, kind="ExternalInput").ap()
            for bn in ("bq", "bk", "bv", "bo", "bf")}
    a_in = nc.dram_tensor("a_in", [64, D], F32, kind="ExternalInput").ap()
    bmat_in = nc.dram_tensor("bmat_in", [D, 64], F32, kind="ExternalInput").ap()
    gw_in = nc.dram_tensor("gw_in", [4, D], F32, kind="ExternalInput").ap()
    gb_in = nc.dram_tensor("gb_in", [1, 4], F32, kind="ExternalInput").ap()
    mkey_in = nc.dram_tensor("mkey_in", [NC * TT, TT], F32, kind="ExternalInput").ap()

    y_out = nc.dram_tensor("y_out", [TOWN, D], F32, kind="ExternalOutput").ap()

    ident16_d = nc.inline_tensor(np.eye(128, dtype=np.float16), "c_id16").ap()
    identbf_d = nc.inline_tensor(np.eye(128, dtype=ml_dtypes.bfloat16), "c_idbf").ap()
    ones16_d = nc.inline_tensor(np.ones((128, 128), dtype=np.float16), "c_o16").ap()
    onesbf_d = nc.inline_tensor(np.ones((128, 128), dtype=ml_dtypes.bfloat16), "c_obf").ap()
    ones32r_d = nc.inline_tensor(np.ones((1, 512), dtype=np.float32), "c_o32").ap()
    lvl_np = np.array([[float(sum(1 for r in RANKS if r <= j))] for j in range(64)],
                      dtype=np.float32)
    lvl_d = nc.inline_tensor(lvl_np, "c_lvl").ap()
    iota4_d = nc.inline_tensor(
        np.tile(np.arange(4, dtype=np.float32)[None, :], (128, 1)), "c_io4").ap()
    eps_d = nc.inline_tensor(np.full((128, 1), EPS, np.float32), "c_eps").ap()

    wloc = {wn: nc.dram_tensor(f"{wn}_loc", [D // NC, D], BF16).ap()
            for wn in ("wq", "wk", "wv", "wo", "wf")}
    wbf = {wn: nc.dram_tensor(f"{wn}_bf", [D, D], BF16).ap()
           for wn in ("wq", "wk", "wv", "wo", "wf")}
    a_bf = nc.dram_tensor("a_bf", [64, D], BF16).ap()
    bpad_bf = nc.dram_tensor("bpad_bf", [D, 128], BF16).ap()
    gwpad_bf = nc.dram_tensor("gwpad_bf", [16, D], BF16).ap()
    ln1d = nc.dram_tensor("ln1d", [TOWN, D], BF16).ap()
    qf_d = nc.dram_tensor("qf_d", [D, TOWN], BF16).ap()
    kf_loc = nc.dram_tensor("kf_loc", [D, TOWN], BF16).ap()
    kf_g = nc.dram_tensor("kf_g", [NC * D, TOWN], BF16).ap()
    vt_loc = nc.dram_tensor("vt_loc", [TOWN, D], FP16).ap()
    vt_g = nc.dram_tensor("vt_g", [T, D], FP16).ap()
    avf_d = nc.dram_tensor("avf_d", [D, TOWN], BF16).ap()
    y1_d = nc.dram_tensor("y1_d", [TOWN, D], BF16).ap()
    x1_d = nc.dram_tensor("x1_d", [TOWN, D], F32).ap()
    ln3d = nc.dram_tensor("ln3d", [TOWN, D], BF16).ap()
    y2_d = nc.dram_tensor("y2_d", [TOWN, D], BF16).ap()
    dram_map = dict(ln1d=ln1d, qf_d=qf_d, kf_loc=kf_loc, kf_g=kf_g, vt_loc=vt_loc,
                    vt_g=vt_g, avf_d=avf_d, y1_d=y1_d, x1_d=x1_d, ln3d=ln3d,
                    y2_d=y2_d, wq_bf=wbf["wq"], wo_bf=wbf["wo"], a_bf=a_bf,
                    bpad_bf=bpad_bf, gwpad_bf=gwpad_bf)
    dbg_names = {}

    ctx = ExitStack()
    with ctx:
        sbg = lambda name, shape, dt: ctx.enter_context(nc.sbuf_tensor(name, shape, dt))
        ident16 = sbg("ident16", [128, 128], FP16)
        identbf = sbg("identbf", [128, 128], BF16)
        ones16 = sbg("ones16", [128, 128], FP16)
        onesbf = sbg("onesbf", [128, 128], BF16)
        ones32r = sbg("ones32r", [1, 512], F32)
        lvl_s = sbg("lvl_s", [64, 1], F32)
        iota4_s = sbg("iota4_s", [128, 4], F32)
        mkey_s = sbg("mkey_s", [128, NC, TT], FP16)
        eps_s = sbg("eps_s", [128, 1], F32)
        nc.const_aps.register(F32, EPS, eps_s[:, 0:1])
        stgb = [sbg(f"stgb{i}", [128, 1024], BF16) for i in range(2)]
        stg16 = [sbg(f"stg16{i}", [128, 1024], FP16) for i in range(2)]

        s_cast = SC(ctx, nc, "s_cast")
        s_wg = SC(ctx, nc, "s_wg")
        s_x = SC(ctx, nc, "s_x")
        s_dv = SC(ctx, nc, "s_dv")
        s_ac = SC(ctx, nc, "s_ac")
        s_st = SC(ctx, nc, "s_st")
        s_wt = SC(ctx, nc, "s_wt")
        s_mm = SC(ctx, nc, "s_mm")
        s_mmt = SC(ctx, nc, "s_mmt")
        s_ev = SC(ctx, nc, "s_ev")
        s_lf = SC(ctx, nc, "s_lf")
        s_kv = SC(ctx, nc, "s_kv")
        s_ad = SC(ctx, nc, "s_ad")
        s_ap = SC(ctx, nc, "s_ap")
        s_ae = SC(ctx, nc, "s_ae")
        s_at = SC(ctx, nc, "s_at")
        s_am = SC(ctx, nc, "s_am")

        blk = ctx.enter_context(nc.Block())

        wg_done = {}
        @blk.gpsimd
        def _(g):
            g.dma_start(ident16[:, :], ident16_d[:, :]).then_inc(*s_cast.inc(16))
            g.dma_start(identbf[:, :], identbf_d[:, :]).then_inc(*s_cast.inc(16))
            g.dma_start(ones16[:, :], ones16_d[:, :]).then_inc(*s_cast.inc(16))
            g.dma_start(onesbf[:, :], onesbf_d[:, :]).then_inc(*s_cast.inc(16))
            g.dma_start(ones32r[:, :], ones32r_d[:, :]).then_inc(*s_cast.inc(16))
            g.dma_start(lvl_s[0:64, :], lvl_d[:, :]).then_inc(*s_cast.inc(16))
            g.dma_start(iota4_s[:, :], iota4_d[:, :]).then_inc(*s_cast.inc(16))
            g.dma_start(eps_s[:, :], eps_d[:, :]).then_inc(*s_cast.inc(16))
            g.dma_start(mkey_s[:, :, :],
                        mkey_in.rearrange("(r p) q -> p r q", p=TT)).then_inc(*s_cast.inc(16))
            cast_at = {}
            for wn in ("wq", "wk", "wv", "wo", "wf"):
                g.dma_start(wloc[wn][:, :], w_sl[wn][:, :]).then_inc(*s_cast.inc(16))
                cast_at[wn] = s_cast.v
            g.dma_start(a_bf[:, :], a_in[:, :]).then_inc(*s_cast.inc(16))
            g.dma_start(bpad_bf[:, 0:64], bmat_in[:, :]).then_inc(*s_cast.inc(16))
            g.dma_start(gwpad_bf[0:4, :], gw_in[:, :]).then_inc(*s_cast.inc(16))
            for wn in ("wq", "wk", "wv", "wo", "wf"):
                g.wait_ge(s_cast.h, cast_at[wn])
                g.collective_compute(
                    "AllGather", ALU.bypass, replica_groups=[list(range(NC))],
                    ins=[wloc[wn].opt()], outs=[wbf[wn].opt()]).then_inc(*s_wg.inc(1))
                wg_done[wn] = s_wg.v
        consts_done = s_cast.v

        def ln_stats_and_normalize(v, x_, out_, xx, stat, dv_b, ac_wait):
            v.reduce_sum(stat[:, 0:1], x_[:, :], axis=AX.X).then_inc(*s_dv.inc())
            v.tensor_mul(xx[:, :], x_[:, :], x_[:, :])
            v.reduce_sum(stat[:, 1:2], xx[:, :], axis=AX.X).then_inc(*s_dv.inc())
            ia = v.tensor_scalar(stat[:, 2:3], stat[:, 0:1], 1.0 / D, None, ALU.mult)
            ia._wait_ge(s_dv.h, dv_b + 2)
            ia.then_inc(*s_dv.inc())
            ib = v.tensor_scalar(stat[:, 4:5], stat[:, 1:2], 1.0 / D, None, ALU.mult)
            ib.then_inc(*s_dv.inc())
            i3 = v.tensor_mul(stat[:, 3:4], stat[:, 2:3], stat[:, 2:3])
            i3._wait_ge(s_dv.h, dv_b + 4)
            i3.then_inc(*s_dv.inc())
            i4 = v.tensor_sub(stat[:, 5:6], stat[:, 4:5], stat[:, 3:4])
            i4._wait_ge(s_dv.h, dv_b + 5)
            i4.then_inc(*s_dv.inc())
            v.wait_ge(*ac_wait)
            v.tensor_mul(stat[:, 7:8], stat[:, 2:3], stat[:, 6:7]).then_inc(*s_dv.inc())
            i7 = v.tensor_scalar(stat[:, 8:9], stat[:, 7:8], -1.0, None, ALU.mult)
            i7._wait_ge(s_dv.h, dv_b + 7)
            i7.then_inc(*s_dv.inc())
            ins = v.tensor_scalar(out_[:, :], x_[:, :], stat[:, 6:7], stat[:, 8:9],
                                  ALU.mult, ALU.add)
            ins._wait_ge(s_dv.h, dv_b + 8)
            ins.then_inc(*s_dv.inc())
            assert s_dv.v == dv_b + 9

        def ln_act_rstd(e, stat, dv_wait):
            e.wait_ge(s_dv.h, dv_wait)
            e.activation(stat[:, 9:10], stat[:, 5:6], AF.Ln, bias=EPS).then_inc(*s_ac.inc())
            i2 = e.activation(stat[:, 6:7], stat[:, 9:10], AF.Exp, scale=-0.5)
            i2._wait_ge(s_ac.h, s_ac.v)
            i2.then_inc(*s_ac.inc())

        if phase_limit >= 1:
            pctx = ExitStack()
            sb = lambda n, sh, dt: pctx.enter_context(nc.sbuf_tensor(n, sh, dt))
            xt = [sb(f"p1xt{i}", [128, D], F32) for i in range(2)]
            xx = sb("p1xx", [128, D], F32)
            lt = [sb(f"p1lt{i}", [128, D], BF16) for i in range(2)]
            stat = sb("p1stat", [128, 12], F32)
            x_b, dv_b, ac_b, st_b = s_x.v, s_dv.v, s_ac.v, s_st.v

            @blk.sync
            def _(s):
                for j in range(NT):
                    if P1_DBG == 3:
                        if j >= 1:
                            s.wait_ge(s_dv.h, dv_b + 9 * (j - 1) + 9)
                    elif j >= 2:
                        s.wait_ge(s_dv.h, dv_b + 9 * (j - 2) + 9)
                    s.dma_start(xt[j % 2][:, :], x_sl[j * TT:(j + 1) * TT, :]
                                ).then_inc(*s_x.inc(16))

            @blk.vector
            def _(v):
                for j in range(NT):
                    v.wait_ge(s_x.h, x_b + 16 * (j + 1))
                    if j >= 2:
                        v.wait_ge(s_st.h, st_b + 16 * (j - 1))
                    if P1_DBG == 1 and j >= 1:
                        v.wait_ge(s_ac.h, ac_b + 2 * j)
                    if P1_DBG == 2 and j >= 1:
                        v.wait_ge(s_st.h, st_b + 16 * j)
                    ln_stats_and_normalize(v, xt[j % 2], lt[j % 2], xx, stat,
                                           dv_b + 9 * j, (s_ac.h, ac_b + 2 * j + 2))
                v.tensor_copy(stat[:, 10:11], stat[:, 9:10]).then_inc(*s_dv.inc())

            @blk.scalar
            def _(e):
                for j in range(NT):
                    ln_act_rstd(e, stat, dv_b + 9 * j + 6)

            @blk.gpsimd
            def _(g):
                for j in range(NT):
                    g.wait_ge(s_dv.h, dv_b + 9 * j + 10)
                    g.dma_start(ln1d[j * TT:(j + 1) * TT, :], lt[j % 2][:, :]
                                ).then_inc(*s_st.inc(16))
            pctx.close()

        if phase_limit >= 2:
            pctx = ExitStack()
            sb = lambda n, sh, dt: pctx.enter_context(nc.sbuf_tensor(n, sh, dt))
            lnf = sb("p2lnf", [128, NDC, TOWN], BF16)
            wtile = [sb(f"p2wt{i}", [128, 512], BF16) for i in range(2)]
            brow = {bn: sb(f"{bn}_row", [1, D], F32) for bn in ("bq", "bk", "bv")}
            ps = pctx.enter_context(nc.psum_tensor("p2ps", [128, 4096], F32))
            st_b = s_st.v

            @blk.sync
            def _(s):
                for bn in ("bq", "bk", "bv"):
                    s.dma_start(brow[bn][:, :], b_in[bn][:, :]).then_inc(*s_lf.inc(16))
                s.wait_ge(s_st.h, st_b)
                for dc in range(NDC):
                    s.dma_start(lnf[:, dc, :], ln1d[:, dc * 128:(dc + 1) * 128],
                                transpose=True).then_inc(*s_lf.inc(16))
            lf_done = s_lf.v

            fetch_list = ([("wq", fg, dc) for fg in range(8) for dc in range(NDC)] +
                          [("wk", fg, dc) for fg in range(8) for dc in range(NDC)] +
                          [("wv", fc, dc) for fc in range(8) for dc in range(NDC)])
            wt_b, mmt_b, mm_b, ev_b, st2_b = s_wt.v, s_mmt.v, s_mm.v, s_ev.v, s_st.v

            @blk.sync
            def _(s):
                s.wait_ge(s_wg.h, max(wg_done["wq"], wg_done["wk"], wg_done["wv"]))
                for k, (wn, fg, dc) in enumerate(fetch_list):
                    if k >= 2:
                        s.wait_ge(s_mmt.h, mmt_b + (k - 1))
                    s.dma_start(wtile[k % 2][:, :],
                                wbf[wn][fg * 512:(fg + 1) * 512, dc * 128:(dc + 1) * 128],
                                transpose=True).then_inc(*s_wt.inc(16))

            @blk.tensor
            def _(t):
                t.wait_ge(s_lf.h, lf_done)
                t.wait_ge(s_cast.h, consts_done)
                k = 0
                for wi, wn in enumerate(("wq", "wk")):
                    bn = ("bq", "bk")[wi]
                    for fg in range(8):
                        blki = wi * 8 + fg
                        if blki >= 1:
                            t.wait_ge(s_ev.h, ev_b + 4 * blki)
                        for dc in range(NDC):
                            t.wait_ge(s_wt.h, wt_b + 16 * (k + 1))
                            wt_ = wtile[k % 2]
                            last = None
                            for fi in range(4):
                                for th in range(2):
                                    r0 = (fi * 2 + th) * 512
                                    last = t.matmul(ps[:, r0:r0 + 512],
                                                    wt_[:, fi * 128:(fi + 1) * 128],
                                                    lnf[:, dc, th * 512:(th + 1) * 512],
                                                    start=(dc == 0), stop=False)
                            last.then_inc(*s_mmt.inc())
                            k += 1
                        last = None
                        for fi in range(4):
                            for th in range(2):
                                r0 = (fi * 2 + th) * 512
                                last = t.matmul(ps[:, r0:r0 + 512],
                                                brow[bn][0:1, fg * 512 + fi * 128:
                                                         fg * 512 + (fi + 1) * 128],
                                                ones32r[0:1, 0:512],
                                                start=False, stop=True)
                        last.then_inc(*s_mm.inc())
                for fc in range(8):
                    t.wait_ge(s_ev.h, ev_b + 64 + 8 * fc)
                    for dc in range(NDC):
                        t.wait_ge(s_wt.h, wt_b + 16 * (k + 1))
                        wt_ = wtile[k % 2]
                        last = None
                        for tt in range(8):
                            last = t.matmul(ps[:, tt * 512:(tt + 1) * 512],
                                            lnf[:, dc, tt * 128:(tt + 1) * 128],
                                            wt_[:, :], start=(dc == 0), stop=False)
                        last.then_inc(*s_mmt.inc())
                        k += 1
                    last = None
                    for tt in range(8):
                        last = t.matmul(ps[:, tt * 512:(tt + 1) * 512],
                                        ones32r[0:1, 0:128],
                                        brow["bv"][0:1, fc * 512:(fc + 1) * 512],
                                        start=False, stop=True)
                    last.then_inc(*s_mm.inc())

            @blk.scalar
            def _(e):
                nev = 0
                for wi in range(2):
                    scale = QSCALE if wi == 0 else 1.0
                    for fg in range(8):
                        blki = wi * 8 + fg
                        e.wait_ge(s_mm.h, mm_b + blki + 1)
                        for fi in range(4):
                            if nev >= 2:
                                e.wait_ge(s_st.h, st2_b + 16 * (nev - 1))
                            r0 = (fi * 2) * 512
                            e.activation(stgb[nev % 2][:, 0:512], ps[:, r0:r0 + 512],
                                         AF.Identity, scale=scale)
                            e.activation(stgb[nev % 2][:, 512:1024],
                                         ps[:, r0 + 512:r0 + 1024],
                                         AF.Identity, scale=scale).then_inc(*s_ev.inc())
                            nev += 1
                for fc in range(8):
                    e.wait_ge(s_mm.h, mm_b + 16 + fc + 1)
                    for tt in range(8):
                        if nev >= 2:
                            e.wait_ge(s_st.h, st2_b + 16 * (nev - 1))
                        e.activation(stg16[nev % 2][:, 0:512],
                                     ps[:, tt * 512:(tt + 1) * 512],
                                     AF.Identity).then_inc(*s_ev.inc())
                        nev += 1
                e.activation(stg16[0][:, 1023:1024], stg16[0][:, 1023:1024],
                             AF.Identity).then_inc(*s_ev.inc())

            @blk.gpsimd
            def _(g):
                nev = 0
                for wi in range(2):
                    dst = qf_d if wi == 0 else kf_loc
                    for fg in range(8):
                        for fi in range(4):
                            g.wait_ge(s_ev.h, ev_b + nev + 2)
                            f0 = fg * 512 + fi * 128
                            g.dma_start(dst[f0:f0 + 128, :], stgb[nev % 2][:, :]
                                        ).then_inc(*s_st.inc(16))
                            nev += 1
                g.wait_ge(s_st.h, s_st.v)
                g.collective_compute("AllGather", ALU.bypass,
                                     replica_groups=[list(range(NC))],
                                     ins=[kf_loc.opt()], outs=[kf_g.opt()]
                                     ).then_inc(*s_kv.inc(1))
                for fc in range(8):
                    for tt in range(8):
                        g.wait_ge(s_ev.h, ev_b + nev + 2)
                        g.dma_start(vt_loc[tt * TT:(tt + 1) * TT, fc * 512:(fc + 1) * 512],
                                    stg16[nev % 2][:, 0:512]).then_inc(*s_st.inc(16))
                        nev += 1
                g.wait_ge(s_st.h, s_st.v)
                g.collective_compute("AllGather", ALU.bypass,
                                     replica_groups=[list(range(NC))],
                                     ins=[vt_loc.opt()], outs=[vt_g.opt()]
                                     ).then_inc(*s_kv.inc(1))
            pctx.close()

        if phase_limit >= 3:
            pctx = ExitStack()
            sb = lambda n, sh, dt: pctx.enter_context(nc.sbuf_tensor(n, sh, dt))
            stbuf = [sb("p3st0", [128, 32, 512], FP16), sb("p3st1", [128, 64, 512], FP16)]
            kf_h = [sb(f"p3kf{i}", [128, 64, 128], BF16) for i in range(2)]
            vt_h = [sb(f"p3vt{i}", [128, 64, 128], FP16) for i in range(2)]
            qf_h = [sb(f"p3qf{i}", [128, 1024], BF16) for i in range(2)]
            oscr = sb("p3oscr", [128, 8, 512], FP16)
            maxt = sb("p3maxt", [128, 512], FP16)
            mbc = sb("p3mbc", [128, 512], FP16)
            rbc = sb("p3rbc", [128, 512], F32)
            mq4 = sb("p3mq4", [128, 4], F32)
            rrow = sb("p3rrow", [1, 512], F32)
            ediag4 = sb("p3ed", [128, 4, 128], FP16)
            avst = [sb(f"p3av{i}", [128, 512], BF16) for i in range(2)]
            st_ps = pctx.enter_context(nc.psum_tensor("p3stps", [128, 2048], F32))
            av_ps = pctx.enter_context(nc.psum_tensor("p3avps", [128, 512], F32))
            l_ps = pctx.enter_context(nc.psum_tensor("p3lps", [1, 512], F32))
            mb_ps = pctx.enter_context(nc.psum_tensor("p3mbps", [128, 512], F32))
            pst_ps = pctx.enter_context(nc.psum_tensor("p3pst", [128, 512], FP16))

            units = _p3_units(n_heads)
            NU = len(units)
            kv_need = s_kv.v
            ad_b = s_ad.v
            M = _p3_marks(n_heads, s_ap.v, s_ae.v, s_at.v, s_st.v)

            @blk.sync
            def _(s):
                s.wait_ge(s_kv.h, kv_need - 1)
                for h in range(n_heads):
                    if h >= 2:
                        s.wait_ge(s_ap.h, M[("avg", 2 * (h - 2) + 1)])
                    hb = h % 2
                    for r in range(NC):
                        src = kf_g[r * D + h * 128: r * D + (h + 1) * 128, :]
                        src = src.rearrange("p (j k) -> p j k", k=128)
                        dst = kf_h[hb].rearrange("p (j r) k -> p j r k", r=8)[:, :, r, :]
                        s.dma_start(dst, src).then_inc(*s_ad.inc(16))
                    s.dma_start(qf_h[hb][:, :], qf_d[h * 128:(h + 1) * 128, :]
                                ).then_inc(*s_ad.inc(16))
                    if h == 0:
                        s.wait_ge(s_kv.h, kv_need)
                    for r in range(NC):
                        src = vt_g[r * 1024:(r + 1) * 1024, h * 128:(h + 1) * 128]
                        src = src.rearrange("(j p) k -> p j k", p=128)
                        dst = vt_h[hb].rearrange("p (j r) k -> p j r k", r=8)[:, :, r, :]
                        s.dma_start(dst, src).then_inc(*s_ad.inc(16))

            gkb0 = {}
            _g = 0
            for u in range(NU):
                gkb0[u] = _g
                _g += _nkb(units[u][1])

            @blk.tensor
            def _(t):
                for e in _p3_pe_events(n_heads):
                    if e[0] == "kb":
                        _, u, kb = e
                        h, half = units[u]
                        hb = h % 2
                        if kb == 0:
                            t.wait_ge(s_ad.h, ad_b + 16 * 17 * (h + 1))
                            t.wait_ge(s_kv.h, kv_need - 1)
                            if u >= 2:
                                t.wait_ge(s_ap.h, M[("avg", u - 2)])
                        g = gkb0[u] + kb
                        off = _off(half, kb)
                        slot = g % 4
                        if g >= 4:
                            t.wait_ge(s_ae.h, M["evacv"][g - 4])
                        t.matmul(st_ps[:, slot * 512 + off:(slot + 1) * 512],
                                 kf_h[hb][:, kb, :],
                                 qf_h[hb][:, half * 512 + off:(half + 1) * 512],
                                 start=True, stop=True).then_inc(*s_ap.inc())
                        assert s_ap.v == M[("kbv", u, kb)]
                    elif e[0] == "T":
                        u = e[1]
                        t.wait_ge(s_at.h, M[("max", u)])
                        last = None
                        for i in range(4):
                            last = t.transpose(pst_ps[:, i * 128:(i + 1) * 128],
                                               maxt[:, i * 128:(i + 1) * 128],
                                               ident16[:, :])
                        last.then_inc(*s_ap.inc())
                        assert s_ap.v == M[("tposed", u)]
                    elif e[0] == "mb":
                        u = e[1]
                        t.wait_ge(s_at.h, M[("ediag", u)])
                        last = None
                        for i in range(4):
                            last = t.matmul(mb_ps[:, i * 128:(i + 1) * 128],
                                            ones16[:, :], ediag4[:, i, :],
                                            start=True, stop=True)
                        last.then_inc(*s_ap.inc())
                        assert s_ap.v == M[("mbmm", u)]
                    elif e[0] == "AV":
                        u = e[1]
                        h, half = units[u]
                        hb = h % 2
                        nkb = _nkb(half)
                        sbuf_ = stbuf[half]
                        t.wait_ge(s_ae.h, M[("exp", u)])
                        t.wait_ge(s_kv.h, kv_need)
                        if u >= 1:
                            t.wait_ge(s_at.h, M[("avscaled", u - 1)])
                        if u >= 2:
                            t.wait_ge(s_st.h, M[("avstored", u - 2)])
                        last = None
                        for kb in range(nkb):
                            off = _off(half, kb)
                            t.matmul(av_ps[:, off:512], vt_h[hb][:, kb, :],
                                     sbuf_[:, kb, off:512], start=(kb == 0),
                                     stop=(kb == nkb - 1), skip_group_check=True)
                            last = t.matmul(l_ps[0:1, off:512], ones16[:, 0:1],
                                            sbuf_[:, kb, off:512], start=(kb == 0),
                                            stop=(kb == nkb - 1), skip_group_check=True)
                        last.then_inc(*s_ap.inc())
                        assert s_ap.v == M[("avg", u)]
                    elif e[0] == "rbc":
                        u = e[1]
                        t.wait_ge(s_at.h, M[("recip", u)])
                        t.matmul(mb_ps[:, 0:512], ones32r[0:1, 0:128], rrow[0:1, :],
                                 start=True, stop=True).then_inc(*s_ap.inc())
                        assert s_ap.v == M[("rbcmm", u)]
                assert s_ap.v == M["ap_end"]

            @blk.scalar
            def _(e):
                for u in range(NU):
                    h, half = units[u]
                    nkb = _nkb(half)
                    sbuf_ = stbuf[half]
                    if u >= 2:
                        e.wait_ge(s_ap.h, M[("avg", u - 2)])
                    for kb in range(nkb):
                        off = _off(half, kb)
                        slot = (gkb0[u] + kb) % 4
                        e.wait_ge(s_ap.h, M[("kbv", u, kb)])
                        e.activation(sbuf_[:, kb, off:512],
                                     st_ps[:, slot * 512 + off:(slot + 1) * 512],
                                     AF.Identity).then_inc(*s_ae.inc())
                    assert s_ae.v == M[("evac", u)]
                    e.wait_ge(s_ap.h, M[("mbmm", u)])
                    e.activation(mbc[:, :], mb_ps[:, :], AF.Identity).then_inc(*s_ae.inc())
                    e.wait_ge(s_at.h, M[("sub", u)])
                    flat = sbuf_.rearrange("p a b -> p (a b)")
                    e.activation(flat, flat, AF.Exp).then_inc(*s_ae.inc())
                    assert s_ae.v == M[("exp", u)]

            av_shield = {}
            @blk.vector
            def _(v):
                v.wait_ge(s_cast.h, consts_done)
                for u in range(NU):
                    h, half = units[u]
                    nkb = _nkb(half)
                    no = nkb // 8
                    sbuf_ = stbuf[half]
                    if u >= 1:
                        v.wait_ge(s_at.h, M[("max", u - 1)])
                        v.wait_ge(s_ap.h, M[("tposed", u - 1)])
                    for o in range(no):
                        off = _off(half, 8 * o)
                        v.wait_ge(s_ae.h, M[("evbase", u)] + 8 * (o + 1))
                        mask_edge = None
                        if _masked_octet(half, o):
                            sl = sbuf_[:, 8 * o:8 * o + 8, off:off + 128]
                            v.tensor_add(sl, sl, mkey_s[:, :, :]).then_inc(*s_am.inc())
                            mask_edge = s_am.v
                        if off > 0:
                            v.memset(oscr[:, o, 0:off], -60000.0)
                        red = v.reduce_max(
                            oscr[:, o, off:512],
                            sbuf_[:, 8 * o:8 * o + 8, off:512].rearrange("p a b -> p b a"),
                            axis=AX.X)
                        if mask_edge is not None:
                            red._wait_ge(s_am.h, mask_edge)
                        if o == no - 1:
                            red.then_inc(*s_at.inc())
                    assert s_at.v == M[("oct", u)]
                    fin = v.reduce_max(maxt[:, :],
                                       oscr[:, 0:no, :].rearrange("p a b -> p b a"),
                                       axis=AX.X)
                    fin._wait_ge(s_at.h, M[("oct", u)])
                    fin.then_inc(*s_at.inc())
                    assert s_at.v == M[("max", u)]
                    v.wait_ge(s_ap.h, M[("tposed", u)])
                    last = None
                    for i in range(4):
                        last = v.reduce_max(mq4[:, i:i + 1],
                                            pst_ps[:, i * 128:(i + 1) * 128], axis=AX.X)
                    last.then_inc(*s_at.inc())
                    assert s_at.v == M[("redge", u)]
                    last = None
                    for i in range(4):
                        ins = v.tensor_scalar(ediag4[:, i, :], ident16[:, :],
                                              mq4[:, i:i + 1], None, ALU.mult)
                        ins._wait_ge(s_at.h, M[("redge", u)])
                        last = ins
                    last.then_inc(*s_at.inc())
                    assert s_at.v == M[("ediag", u)]
                    v.wait_ge(s_ae.h, M[("mbc", u)])
                    bview = mbc.rearrange("p (o q) -> p o q", o=1).broadcast_to(
                        [128, nkb, 512])
                    v.tensor_sub(sbuf_[:, :, :], sbuf_[:, :, :], bview
                                 ).then_inc(*s_at.inc())
                    assert s_at.v == M[("sub", u)]
                    v.wait_ge(s_ap.h, M[("avg", u)])
                    v.reciprocal(rrow[0:1, :], l_ps[0:1, :]).then_inc(*s_at.inc())
                    assert s_at.v == M[("recip", u)]
                    v.wait_ge(s_ap.h, M[("rbcmm", u)])
                    v.tensor_copy(rbc[:, :], mb_ps[:, :]).then_inc(*s_at.inc())
                    assert s_at.v == M[("rbccp", u)]
                    if u >= 2:
                        v.wait_ge(s_st.h, M[("avstored", u - 2)])
                    im2 = v.tensor_mul(avst[u % 2][:, :], av_ps[:, :], rbc[:, :])
                    im2._wait_ge(s_at.h, M[("rbccp", u)])
                    im2.then_inc(*s_at.inc())
                    assert s_at.v == M[("avscaled", u)]
                    v.tensor_copy(mq4[:, 0:1], mq4[:, 1:2]).then_inc(*s_am.inc())
                    av_shield[u] = s_am.v

            @blk.gpsimd
            def _(g):
                for u in range(NU):
                    h, half = units[u]
                    g.wait_ge(s_am.h, av_shield[u])
                    g.dma_start(avf_d[h * 128:(h + 1) * 128, half * 512:(half + 1) * 512],
                                avst[u % 2][:, :]).then_inc(*s_st.inc(16))
                    assert s_st.v == M[("avstored", u)]
            pctx.close()

        def proj_phase(tag, wsrc, bias_src, lhs_loader, lora, dst, wg_need):
            pctx = ExitStack()
            sb = lambda n, sh, dt: pctx.enter_context(nc.sbuf_tensor(n, sh, dt))
            w8 = [sb(f"{tag}w{i}", [128, NDC, 512], BF16) for i in range(2)]
            bias_row = sb(f"{tag}brow", [1, D], F32)
            ps4 = pctx.enter_context(nc.psum_tensor(f"{tag}ps", [128, 2048], F32))
            lhs = lhs_loader(pctx, sb)

            wt_b, mm_b, ev_b, st_b = s_wt.v, s_mm.v, s_ev.v, s_st.v
            @blk.sync
            def _(s):
                s.dma_start(bias_row[:, :], bias_src[:, :]).then_inc(*s_x.inc(16))
            bias_ld = s_x.v

            @blk.sync
            def _(s):
                s.wait_ge(s_wg.h, wg_need)
                for o8 in range(8):
                    if o8 >= 2:
                        s.wait_ge(s_mm.h, mm_b + 8 * (o8 - 1))
                    for dc in range(NDC):
                        s.dma_start(w8[o8 % 2][:, dc, :],
                                    wsrc[o8 * 512:(o8 + 1) * 512, dc * 128:(dc + 1) * 128],
                                    transpose=True).then_inc(*s_wt.inc(16))

            @blk.tensor
            def _(t):
                lhs.pe_wait(t)
                t.wait_ge(s_x.h, bias_ld)
                for o8 in range(8):
                    t.wait_ge(s_wt.h, wt_b + 16 * 32 * (o8 + 1))
                    for tt in range(8):
                        grp = o8 * 8 + tt
                        slot = grp % 4
                        if grp >= 4:
                            t.wait_ge(s_ev.h, ev_b + (grp - 3))
                        for dc in range(NDC):
                            t.matmul(ps4[:, slot * 512:(slot + 1) * 512],
                                     lhs.get(dc, tt), w8[o8 % 2][:, dc, :],
                                     start=(dc == 0), stop=False)
                        if lora is not None:
                            umf_s, btT_s = lora
                            t.matmul(ps4[:, slot * 512:(slot + 1) * 512],
                                     umf_s[0:64, tt * 128:(tt + 1) * 128],
                                     btT_s[0:64, o8 * 512:(o8 + 1) * 512],
                                     start=False, stop=False)
                        t.matmul(ps4[:, slot * 512:(slot + 1) * 512],
                                 ones32r[0:1, 0:128],
                                 bias_row[0:1, o8 * 512:(o8 + 1) * 512],
                                 start=False, stop=True).then_inc(*s_mm.inc())

            @blk.scalar
            def _(e):
                for grp in range(64):
                    slot = grp % 4
                    e.wait_ge(s_mm.h, mm_b + grp + 1)
                    if grp >= 2:
                        e.wait_ge(s_st.h, st_b + 16 * (grp - 1))
                    e.activation(stgb[grp % 2][:, 0:512],
                                 ps4[:, slot * 512:(slot + 1) * 512],
                                 AF.Identity).then_inc(*s_ev.inc())
                e.activation(stgb[0][:, 1023:1024], stgb[0][:, 1023:1024],
                             AF.Identity).then_inc(*s_ev.inc())

            @blk.gpsimd
            def _(g):
                for grp in range(64):
                    o8, tt = grp // 8, grp % 8
                    g.wait_ge(s_ev.h, ev_b + grp + 2)
                    g.dma_start(dst[tt * TT:(tt + 1) * TT, o8 * 512:(o8 + 1) * 512],
                                stgb[grp % 2][:, 0:512]).then_inc(*s_st.inc(16))
            pctx.close()

        if phase_limit >= 4:
            class _AvfLhs:
                def alloc(self, pctx, sb):
                    self.buf = sb("p4avf", [128, NDC, 8, 128], BF16)
                    st_need = s_st.v
                    @blk.sync
                    def _(s):
                        s.wait_ge(s_st.h, st_need)
                        s.dma_start(self.buf[:, :, :, :],
                                    avf_d.rearrange("(c p) (t q) -> p c t q", p=128, q=128)
                                    ).then_inc(*s_x.inc(16))
                    self.need = s_x.v
                    return self

                def pe_wait(self, t):
                    t.wait_ge(s_x.h, self.need)

                def get(self, dc, tt):
                    return self.buf[:, dc, tt, :]

            lhs4 = _AvfLhs()
            proj_phase("p4", wbf["wo"], b_in["bo"],
                       lambda p, s: lhs4.alloc(p, s), None, y1_d, wg_done["wo"])

        if phase_limit >= 5:
            pctx = ExitStack()
            sb = lambda n, sh, dt: pctx.enter_context(nc.sbuf_tensor(n, sh, dt))
            xt = [sb(f"p5xt{i}", [128, D], F32) for i in range(2)]
            y1t = [sb(f"p5y1{i}", [128, D], BF16) for i in range(2)]
            zz = sb("p5zz", [128, D], F32)
            xx = sb("p5xx", [128, D], F32)
            lt = [sb(f"p5lt{i}", [128, D], BF16) for i in range(2)]
            stat = sb("p5stat", [128, 12], F32)
            x_b, dv_b, ac_b, st_b = s_x.v, s_dv.v, s_ac.v, s_st.v
            st_need = s_st.v
            DVT = 20

            @blk.sync
            def _(s):
                s.wait_ge(s_st.h, st_need)
                for j in range(NT):
                    if j >= 2:
                        s.wait_ge(s_dv.h, dv_b + DVT * (j - 2) + DVT)
                        s.wait_ge(s_st.h, st_b + 32 * (j - 1))
                    s.dma_start(y1t[j % 2][:, :], y1_d[j * TT:(j + 1) * TT, :]
                                ).then_inc(*s_x.inc(16))
                    s.dma_start(xt[j % 2][:, :], x_sl[j * TT:(j + 1) * TT, :]
                                ).then_inc(*s_x.inc(16))

            @blk.vector
            def _(v):
                for j in range(NT):
                    y_, x_ = y1t[j % 2], xt[j % 2]
                    v.wait_ge(s_x.h, x_b + 32 * (j + 1))
                    base = dv_b + DVT * j
                    ln_stats_and_normalize(v, y_, zz, xx, stat, base,
                                           (s_ac.h, ac_b + 4 * j + 2))
                    ia = v.tensor_add(x_[:, :], zz[:, :], x_[:, :])
                    ia._wait_ge(s_dv.h, base + 9)
                    ia.then_inc(*s_dv.inc())
                    v.wait_ge(s_dv.h, base + 10)
                    ln_stats_and_normalize(v, x_, lt[j % 2], xx, stat, base + 10,
                                           (s_ac.h, ac_b + 4 * j + 4))
                    v.tensor_copy(stat[:, 10:11], stat[:, 11:12]).then_inc(*s_dv.inc())
                    assert s_dv.v == base + DVT

            @blk.scalar
            def _(e):
                for j in range(NT):
                    base = dv_b + DVT * j
                    ln_act_rstd(e, stat, base + 6)
                    ln_act_rstd(e, stat, base + 10 + 6)

            @blk.gpsimd
            def _(g):
                for j in range(NT):
                    base = dv_b + DVT * j
                    g.wait_ge(s_dv.h, base + 11)
                    g.dma_start(x1_d[j * TT:(j + 1) * TT, :], xt[j % 2][:, :]
                                ).then_inc(*s_st.inc(16))
                    g.wait_ge(s_dv.h, base + 20)
                    g.dma_start(ln3d[j * TT:(j + 1) * TT, :], lt[j % 2][:, :]
                                ).then_inc(*s_st.inc(16))
            pctx.close()

        if phase_limit >= 6:
            pctx6 = ExitStack()
            sb6 = lambda n, sh, dt: pctx6.enter_context(nc.sbuf_tensor(n, sh, dt))
            lnf6 = sb6("p6lnf", [128, NDC, TOWN], BF16)
            at_s = sb6("p6at", [128, NDC, 64], BF16)
            gwT_s = sb6("p6gw", [128, NDC, 16], BF16)
            btT_s = sb6("p6bt", [128, D], BF16)
            umf_s = sb6("p6umf", [64, 1024], BF16)
            maskf_s = sb6("p6mask", [64, 1024], BF16)
            scores_s = sb6("p6sc", [128, 32], F32)
            oh_s = sb6("p6oh", [128, 32], F32)
            ecol_s = sb6("p6ec", [128, 8], F32)
            gst = sb6("p6gst", [128, 8], F32)
            edb = sb6("p6edb", [128, 8, 128], BF16)
            gb_row = sb6("p6gb", [1, 4], F32)
            psg = pctx6.enter_context(nc.psum_tensor("p6psg", [128, 512], F32))
            psu = pctx6.enter_context(nc.psum_tensor("p6psu", [64, 1024], F32))
            pse = pctx6.enter_context(nc.psum_tensor("p6pse", [64, 128], F32))

            st_need6 = s_st.v
            @blk.sync
            def _(s):
                s.wait_ge(s_st.h, st_need6)
                for dc in range(NDC):
                    s.dma_start(lnf6[:, dc, :], ln3d[:, dc * 128:(dc + 1) * 128],
                                transpose=True).then_inc(*s_lf.inc(16))
                s.wait_ge(s_cast.h, consts_done)
                for dc in range(NDC):
                    s.dma_start(at_s[:, dc, :], a_bf[:, dc * 128:(dc + 1) * 128],
                                transpose=True).then_inc(*s_lf.inc(16))
                for dc in range(NDC):
                    s.dma_start(gwT_s[:, dc, :], gwpad_bf[:, dc * 128:(dc + 1) * 128],
                                transpose=True).then_inc(*s_lf.inc(16))
                s.dma_start(btT_s[:, :], bpad_bf[:, :], transpose=True
                            ).then_inc(*s_lf.inc(16))
                s.dma_start(gb_row[:, :], gb_in[:, :]).then_inc(*s_lf.inc(16))
            lf6_done = s_lf.v

            mm6_b, dv6_b = s_mm.v, s_dv.v
            @blk.tensor
            def _(t):
                t.wait_ge(s_lf.h, lf6_done)
                last = None
                for tt in range(8):
                    for dc in range(NDC):
                        t.matmul(psg[:, tt * 4:(tt + 1) * 4],
                                 lnf6[:, dc, tt * 128:(tt + 1) * 128],
                                 gwT_s[:, dc, 0:4], start=(dc == 0), stop=False)
                    last = t.matmul(psg[:, tt * 4:(tt + 1) * 4], ones32r[0:1, 0:128],
                                    gb_row[0:1, 0:4], start=False, stop=True)
                last.then_inc(*s_mm.inc())
                last = None
                for th in range(2):
                    for dc in range(NDC):
                        last = t.matmul(psu[0:64, th * 512:(th + 1) * 512],
                                        at_s[:, dc, :],
                                        lnf6[:, dc, th * 512:(th + 1) * 512],
                                        start=(dc == 0), stop=(dc == NDC - 1))
                last.then_inc(*s_mm.inc())
                for tt in range(8):
                    t.wait_ge(s_dv.h, dv6_b + 5 + tt + 1)
                    if tt >= 1:
                        t.wait_ge(s_dv.h, dv6_b + 13 + tt)
                    t.matmul(pse[0:64, :], onesbf[:, 0:64], edb[:, tt, :],
                             start=True, stop=True).then_inc(*s_mm.inc())

            @blk.vector
            def _(v):
                v.wait_ge(s_mm.h, mm6_b + 1)
                v.tensor_copy(scores_s[:, :], psg[:, 0:32]).then_inc(*s_dv.inc())
                last = None
                for tt in range(8):
                    ins = v.reduce_max(gst[:, tt:tt + 1], scores_s[:, tt * 4:(tt + 1) * 4],
                                       axis=AX.X)
                    ins._wait_ge(s_dv.h, dv6_b + 1)
                    last = ins
                last.then_inc(*s_dv.inc())
                last = None
                for tt in range(8):
                    ins = v.tensor_scalar(oh_s[:, tt * 4:(tt + 1) * 4],
                                          scores_s[:, tt * 4:(tt + 1) * 4],
                                          gst[:, tt:tt + 1], None, ALU.is_ge)
                    ins._wait_ge(s_dv.h, dv6_b + 2)
                    last = ins
                last.then_inc(*s_dv.inc())
                oh3 = oh_s.rearrange("p (a b) -> p a b", b=4)
                ins = v.tensor_mul(oh3, oh3,
                                   iota4_s.rearrange("p (o q) -> p o q", o=1)
                                   .broadcast_to([128, 8, 4]))
                ins._wait_ge(s_dv.h, dv6_b + 3)
                ins.then_inc(*s_dv.inc())
                last = None
                for tt in range(8):
                    ins = v.reduce_sum(ecol_s[:, tt:tt + 1], oh_s[:, tt * 4:(tt + 1) * 4],
                                       axis=AX.X)
                    ins._wait_ge(s_dv.h, dv6_b + 4)
                    last = ins
                last.then_inc(*s_dv.inc())
                for tt in range(8):
                    ins = v.tensor_scalar(edb[:, tt, :], identbf[:, :],
                                          ecol_s[:, tt:tt + 1], None, ALU.mult)
                    ins._wait_ge(s_dv.h, dv6_b + 5)
                    ins.then_inc(*s_dv.inc())
                for tt in range(8):
                    v.wait_ge(s_mm.h, mm6_b + 2 + tt + 1)
                    v.tensor_scalar(maskf_s[0:64, tt * 128:(tt + 1) * 128],
                                    pse[0:64, :], lvl_s[0:64, 0:1], None, ALU.is_ge
                                    ).then_inc(*s_dv.inc())
                v.wait_ge(s_mm.h, mm6_b + 2)
                ins = v.tensor_mul(umf_s[0:64, :], psu[0:64, :], maskf_s[0:64, :])
                ins._wait_ge(s_dv.h, dv6_b + 21)
                ins.then_inc(*s_dv.inc())
            umf_done = s_dv.v

            class _LnfLhs:
                def alloc(self, pctx, sb):
                    return self

                def pe_wait(self, t):
                    t.wait_ge(s_lf.h, lf6_done)
                    t.wait_ge(s_dv.h, umf_done)

                def get(self, dc, tt):
                    return lnf6[:, dc, tt * 128:(tt + 1) * 128]

            proj_phase("p6w", wbf["wf"], b_in["bf"], lambda p, s: _LnfLhs().alloc(p, s),
                       (umf_s, btT_s), y2_d, wg_done["wf"])
            pctx6.close()

        if phase_limit >= 7:
            pctx = ExitStack()
            sb = lambda n, sh, dt: pctx.enter_context(nc.sbuf_tensor(n, sh, dt))
            x1t = [sb(f"p7x{i}", [128, D], F32) for i in range(2)]
            y2t = [sb(f"p7y{i}", [128, D], BF16) for i in range(2)]
            zz = sb("p7zz", [128, D], F32)
            xx = sb("p7xx", [128, D], F32)
            stat = sb("p7stat", [128, 12], F32)
            x_b, dv_b, ac_b, st_b = s_x.v, s_dv.v, s_ac.v, s_st.v
            st_need = s_st.v
            DVT = 10

            @blk.sync
            def _(s):
                s.wait_ge(s_st.h, st_need)
                for j in range(NT):
                    if j >= 2:
                        s.wait_ge(s_dv.h, dv_b + DVT * (j - 2) + DVT)
                        s.wait_ge(s_st.h, st_b + 16 * (j - 1))
                    s.dma_start(y2t[j % 2][:, :], y2_d[j * TT:(j + 1) * TT, :]
                                ).then_inc(*s_x.inc(16))
                    s.dma_start(x1t[j % 2][:, :], x1_d[j * TT:(j + 1) * TT, :]
                                ).then_inc(*s_x.inc(16))

            @blk.vector
            def _(v):
                for j in range(NT):
                    y_, x_ = y2t[j % 2], x1t[j % 2]
                    v.wait_ge(s_x.h, x_b + 32 * (j + 1))
                    base = dv_b + DVT * j
                    ln_stats_and_normalize(v, y_, zz, xx, stat, base,
                                           (s_ac.h, ac_b + 2 * j + 2))
                    ia = v.tensor_add(x_[:, :], zz[:, :], x_[:, :])
                    ia._wait_ge(s_dv.h, base + 9)
                    ia.then_inc(*s_dv.inc())
                    assert s_dv.v == base + DVT
                v.tensor_copy(stat[:, 10:11], stat[:, 9:10]).then_inc(*s_dv.inc())

            @blk.scalar
            def _(e):
                for j in range(NT):
                    ln_act_rstd(e, stat, dv_b + DVT * j + 6)

            @blk.gpsimd
            def _(g):
                for j in range(NT):
                    g.wait_ge(s_dv.h, dv_b + DVT * j + 11)
                    g.dma_start(y_out[j * TT:(j + 1) * TT, :], x1t[j % 2][:, :]
                                ).then_inc(*s_st.inc(16))
            pctx.close()

        if phase_limit == 2 and "kf_g2" in taps:
            kf_g2 = nc.dram_tensor("kf_g2", [NC * D, TOWN], BF16).ap()
            dram_map["kf_g2"] = kf_g2
            @blk.gpsimd
            def _(g):
                g.wait_ge(s_kv.h, s_kv.v)
                g.collective_compute("AllGather", ALU.bypass,
                                     replica_groups=[list(range(NC))],
                                     ins=[kf_loc.opt()], outs=[kf_g2.opt()]
                                     ).then_inc(*s_kv.inc(1))
                g.wait_ge(s_kv.h, s_kv.v)

        @blk.gpsimd
        def _(g):
            g.wait_ge(s_st.h, s_st.v)
            g.wait_ge(s_kv.h, s_kv.v)
            for name in taps:
                src = dram_map[name]
                out = nc.dram_tensor(f"dbg_{name}", list(src.shape), src.dtype,
                                     kind="ExternalOutput").ap()
                dbg_names[name] = f"dbg_{name}"
                g.dma_start(out, src).then_inc(*s_st.inc(16))
            g.wait_ge(s_st.h, s_st.v)

    return nc, dbg_names


def _mkey_for_core(c):
    m = np.full((NC * TT, TT), NEGBIG, dtype=np.float32)
    for r in range(NC):
        if r < c:
            m[r * TT:(r + 1) * TT, :] = 0.0
        elif r == c:
            kk = np.arange(TT)[:, None]
            qq = np.arange(TT)[None, :]
            m[r * TT:(r + 1) * TT, :] = np.where(kk <= qq, 0.0, NEGBIG)
    return m


def make_in_maps(inputs):
    f32 = lambda a: np.ascontiguousarray(np.asarray(a, dtype=np.float32))
    x = f32(inputs["L0_attn_x"]).reshape(T, D)
    W = {"wq": f32(inputs["L0_attn_Wq"]), "wk": f32(inputs["L0_attn_Wk"]),
         "wv": f32(inputs["L0_attn_Wv"]), "wo": f32(inputs["L0_attn_Wo"]),
         "wf": f32(inputs["L0_ffn_W"])}
    B = {"bq": f32(inputs["L0_attn_bq"]), "bk": f32(inputs["L0_attn_bk"]),
         "bv": f32(inputs["L0_attn_bv"]), "bo": f32(inputs["L0_attn_bo"]),
         "bf": f32(inputs["L0_ffn_b"])}
    A = f32(inputs["L0_ffn_A"])
    Bm = f32(inputs["L0_ffn_B"])
    gw = f32(inputs["L0_ffn_gate_w"])
    gb = f32(inputs["L0_ffn_gate_b"])
    xr = x.reshape(T // TT, TT, D)
    in_maps = []
    for c in range(NC):
        m = {"x_sl": np.ascontiguousarray(xr[c::NC].reshape(TOWN, D)),
             "a_in": A, "bmat_in": Bm, "gw_in": gw,
             "gb_in": gb.reshape(1, 4),
             "mkey_in": _mkey_for_core(c)}
        for wn in W:
            m[f"{wn}_sl"] = np.ascontiguousarray(W[wn][c * (D // NC):(c + 1) * (D // NC)])
        for bn in B:
            m[f"{bn}_in"] = B[bn].reshape(1, D)
        in_maps.append(m)
    return in_maps


def assemble_output(results):
    y = np.empty((T, D), dtype=np.float32)
    yr = y.reshape(T // TT, TT, D)
    for c in range(NC):
        yr[c::NC] = results[c]["y_out"].reshape(NT, TT, D)
    return y.reshape(1, T, D)


def kernel(**inputs):
    nc, _ = build_program()
    in_maps = make_in_maps(inputs)
    res = run_bass_kernel_spmd(nc, in_maps, core_ids=list(range(NC)))
    return assemble_output(res.results)


# version 03
# speedup vs baseline: 1.4137x; 1.4137x over previous
import math
import numpy as np
import ml_dtypes
from contextlib import ExitStack

import concourse.bass as bass
import concourse.mybir as mybir
from concourse.bass_utils import run_bass_kernel_spmd

F32 = mybir.dt.float32
BF16 = mybir.dt.bfloat16
FP16 = mybir.dt.float16
AF = mybir.ActivationFunctionType
ALU = mybir.AluOpType
AX = mybir.AxisListType

D = 4096
T = 8192
NC = 8
NT = 8
TT = 128
TOWN = NT * TT
H = 32
HD = 128
NDC = 32
EPS = 1e-5
QSCALE = 1.0 / math.sqrt(HD)
RANKS = (8, 16, 32, 64)
NEGBIG = -30000.0
P1_DBG = 0
P3_LOOKAHEAD = 1


class SC:
    def __init__(self, ctx, nc, name):
        self.h = ctx.enter_context(nc.semaphore(name))
        self.v = 0

    def inc(self, n=1):
        self.v += n
        return self.h, n


def _p3_units(n_heads):
    return [(h, half) for h in range(n_heads) for half in range(2)]


def _nkb(half):
    return 32 if half == 0 else 64


def _off(half, kb):
    js = (kb // 8) if half == 0 else max(0, kb // 8 - 4)
    return js * 128


def _masked_octet(half, o):
    return True if half == 0 else (o >= 4)


def _p3_pe_events(n_heads):
    units = _p3_units(n_heads)
    NU = len(units)
    ev = []
    for kb in range(_nkb(units[0][1])):
        ev.append(("kb", 0, kb))
    for u in range(NU):
        if u + 1 < NU:
            for kb in range(4):
                ev.append(("kb", u + 1, kb))
        ev.append(("T", u))
        ev.append(("mb", u))
        if u + 1 < NU:
            for kb in range(4, _nkb(units[u + 1][1])):
                ev.append(("kb", u + 1, kb))
        ev.append(("AV", u))
        ev.append(("rbc", u))
    return ev


def _p3_act_events(n_heads):
    NU = 2 * n_heads
    ev = [("evacs", 0), ("mbc", 0)]
    for u in range(NU):
        if u + 1 < NU:
            ev.append(("evacs", u + 1))
        ev.append(("exp", u))
        if u + 1 < NU:
            ev.append(("mbc", u + 1))
    return ev


def _p3_dve_events(n_heads):
    NU = 2 * n_heads
    ev = [("oct", 0), ("mq", 0), ("sub", 0)]
    for u in range(NU):
        if u + 1 < NU:
            ev.append(("oct", u + 1))
        ev.append(("fin", u))
        if u + 1 < NU:
            ev.append(("mq", u + 1))
            ev.append(("sub", u + 1))
    return ev


def _p3_marks(n_heads, ap0, ae0, at0, st0):
    m = {}
    units = _p3_units(n_heads)
    NU = len(units)
    ap = ap0
    for e in _p3_pe_events(n_heads):
        ap += 1
        if e[0] == "kb":
            m[("kbv", e[1], e[2])] = ap
        elif e[0] == "T":
            m[("tposed", e[1])] = ap
        elif e[0] == "mb":
            m[("mbmm", e[1])] = ap
        elif e[0] == "AV":
            m[("avg", e[1])] = ap
        elif e[0] == "rbc":
            m[("rbcmm", e[1])] = ap
    m["ap_end"] = ap
    ae = ae0
    evacv = {}
    gk = {}
    _g = 0
    for u in range(NU):
        gk[u] = _g
        _g += _nkb(units[u][1])
    for e in _p3_act_events(n_heads):
        kind, u = e
        if kind == "evacs":
            m[("evbase", u)] = ae
            for kb in range(_nkb(units[u][1])):
                ae += 1
                evacv[gk[u] + kb] = ae
            m[("evac", u)] = ae
        elif kind == "mbc":
            ae += 1; m[("mbc", u)] = ae
        elif kind == "exp":
            ae += 1; m[("exp", u)] = ae
    m["ae_end"] = ae
    m["evacv"] = evacv
    at = at0
    for e in _p3_dve_events(n_heads):
        kind, u = e
        if kind == "oct":
            at += 1; m[("oct", u)] = at
            at += 1; m[("max", u)] = at
        elif kind == "mq":
            at += 1; m[("redge", u)] = at
            at += 1; m[("ediag", u)] = at
        elif kind == "sub":
            at += 1; m[("sub", u)] = at
        elif kind == "fin":
            at += 1; m[("recip", u)] = at
            at += 1; m[("rbccp", u)] = at
            at += 1; m[("avscaled", u)] = at
    m["at_end"] = at
    st = st0
    for u in range(NU):
        st += 16; m[("avstored", u)] = st
    m["st_end"] = st
    return m


def build_program(n_heads=H, phase_limit=99, taps=()):
    nc = bass.Bass("TRN2", target_bir_lowering=False)

    x_sl = nc.dram_tensor("x_sl", [TOWN, D], F32, kind="ExternalInput").ap()
    w_sl = {wn: nc.dram_tensor(f"{wn}_sl", [D // NC, D], F32, kind="ExternalInput").ap()
            for wn in ("wq", "wk", "wv", "wo", "wf")}
    b_in = {bn: nc.dram_tensor(f"{bn}_in", [1, D], F32, kind="ExternalInput").ap()
            for bn in ("bq", "bk", "bv", "bo", "bf")}
    a_in = nc.dram_tensor("a_in", [64, D], F32, kind="ExternalInput").ap()
    bmat_in = nc.dram_tensor("bmat_in", [D, 64], F32, kind="ExternalInput").ap()
    gw_in = nc.dram_tensor("gw_in", [4, D], F32, kind="ExternalInput").ap()
    gb_in = nc.dram_tensor("gb_in", [1, 4], F32, kind="ExternalInput").ap()
    mkey_in = nc.dram_tensor("mkey_in", [NC * TT, TT], F32, kind="ExternalInput").ap()

    y_out = nc.dram_tensor("y_out", [TOWN, D], F32, kind="ExternalOutput").ap()

    ident16_d = nc.inline_tensor(np.eye(128, dtype=np.float16), "c_id16").ap()
    identbf_d = nc.inline_tensor(np.eye(128, dtype=ml_dtypes.bfloat16), "c_idbf").ap()
    ones16_d = nc.inline_tensor(np.ones((128, 128), dtype=np.float16), "c_o16").ap()
    onesbf_d = nc.inline_tensor(np.ones((128, 128), dtype=ml_dtypes.bfloat16), "c_obf").ap()
    ones32r_d = nc.inline_tensor(np.ones((1, 512), dtype=np.float32), "c_o32").ap()
    lvl_np = np.array([[float(sum(1 for r in RANKS if r <= j))] for j in range(64)],
                      dtype=np.float32)
    lvl_d = nc.inline_tensor(lvl_np, "c_lvl").ap()
    iota4_d = nc.inline_tensor(
        np.tile(np.arange(4, dtype=np.float32)[None, :], (128, 1)), "c_io4").ap()
    eps_d = nc.inline_tensor(np.full((128, 1), EPS, np.float32), "c_eps").ap()

    wloc = {wn: nc.dram_tensor(f"{wn}_loc", [D // NC, D], BF16).ap()
            for wn in ("wq", "wk", "wv", "wo", "wf")}
    wbf = {wn: nc.dram_tensor(f"{wn}_bf", [D, D], BF16).ap()
           for wn in ("wq", "wk", "wv", "wo", "wf")}
    a_bf = nc.dram_tensor("a_bf", [64, D], BF16).ap()
    bpad_bf = nc.dram_tensor("bpad_bf", [D, 128], BF16).ap()
    gwpad_bf = nc.dram_tensor("gwpad_bf", [16, D], BF16).ap()
    ln1d = nc.dram_tensor("ln1d", [TOWN, D], BF16).ap()
    qf_d = nc.dram_tensor("qf_d", [D, TOWN], BF16).ap()
    kf_loc = nc.dram_tensor("kf_loc", [D, TOWN], BF16).ap()
    kf_g = nc.dram_tensor("kf_g", [NC * D, TOWN], BF16).ap()
    vt_loc = nc.dram_tensor("vt_loc", [TOWN, D], FP16).ap()
    vt_g = nc.dram_tensor("vt_g", [T, D], FP16).ap()
    avf_d = nc.dram_tensor("avf_d", [D, TOWN], BF16).ap()
    y1_d = nc.dram_tensor("y1_d", [TOWN, D], BF16).ap()
    x1_d = nc.dram_tensor("x1_d", [TOWN, D], F32).ap()
    ln3d = nc.dram_tensor("ln3d", [TOWN, D], BF16).ap()
    y2_d = nc.dram_tensor("y2_d", [TOWN, D], BF16).ap()
    dram_map = dict(ln1d=ln1d, qf_d=qf_d, kf_loc=kf_loc, kf_g=kf_g, vt_loc=vt_loc,
                    vt_g=vt_g, avf_d=avf_d, y1_d=y1_d, x1_d=x1_d, ln3d=ln3d,
                    y2_d=y2_d, wq_bf=wbf["wq"], wo_bf=wbf["wo"], a_bf=a_bf,
                    bpad_bf=bpad_bf, gwpad_bf=gwpad_bf)
    dbg_names = {}

    ctx = ExitStack()
    with ctx:
        sbg = lambda name, shape, dt: ctx.enter_context(nc.sbuf_tensor(name, shape, dt))
        ident16 = sbg("ident16", [128, 128], FP16)
        identbf = sbg("identbf", [128, 128], BF16)
        ones16 = sbg("ones16", [128, 128], FP16)
        onesbf = sbg("onesbf", [128, 128], BF16)
        ones32r = sbg("ones32r", [1, 512], F32)
        lvl_s = sbg("lvl_s", [64, 1], F32)
        iota4_s = sbg("iota4_s", [128, 4], F32)
        mkey_s = sbg("mkey_s", [128, NC, TT], FP16)
        eps_s = sbg("eps_s", [128, 1], F32)
        nc.const_aps.register(F32, EPS, eps_s[:, 0:1])
        stgb = [sbg(f"stgb{i}", [128, 1024], BF16) for i in range(2)]
        stg16 = [sbg(f"stg16{i}", [128, 1024], FP16) for i in range(2)]

        s_cast = SC(ctx, nc, "s_cast")
        s_wg = SC(ctx, nc, "s_wg")
        s_x = SC(ctx, nc, "s_x")
        s_dv = SC(ctx, nc, "s_dv")
        s_ac = SC(ctx, nc, "s_ac")
        s_st = SC(ctx, nc, "s_st")
        s_wt = SC(ctx, nc, "s_wt")
        s_mm = SC(ctx, nc, "s_mm")
        s_mmt = SC(ctx, nc, "s_mmt")
        s_ev = SC(ctx, nc, "s_ev")
        s_lf = SC(ctx, nc, "s_lf")
        s_kv = SC(ctx, nc, "s_kv")
        s_ad = SC(ctx, nc, "s_ad")
        s_ap = SC(ctx, nc, "s_ap")
        s_ae = SC(ctx, nc, "s_ae")
        s_at = SC(ctx, nc, "s_at")
        s_am = SC(ctx, nc, "s_am")

        blk = ctx.enter_context(nc.Block())

        wg_done = {}
        @blk.gpsimd
        def _(g):
            g.dma_start(ident16[:, :], ident16_d[:, :]).then_inc(*s_cast.inc(16))
            g.dma_start(identbf[:, :], identbf_d[:, :]).then_inc(*s_cast.inc(16))
            g.dma_start(ones16[:, :], ones16_d[:, :]).then_inc(*s_cast.inc(16))
            g.dma_start(onesbf[:, :], onesbf_d[:, :]).then_inc(*s_cast.inc(16))
            g.dma_start(ones32r[:, :], ones32r_d[:, :]).then_inc(*s_cast.inc(16))
            g.dma_start(lvl_s[0:64, :], lvl_d[:, :]).then_inc(*s_cast.inc(16))
            g.dma_start(iota4_s[:, :], iota4_d[:, :]).then_inc(*s_cast.inc(16))
            g.dma_start(eps_s[:, :], eps_d[:, :]).then_inc(*s_cast.inc(16))
            g.dma_start(mkey_s[:, :, :],
                        mkey_in.rearrange("(r p) q -> p r q", p=TT)).then_inc(*s_cast.inc(16))
            cast_at = {}
            for wn in ("wq", "wk", "wv", "wo", "wf"):
                g.dma_start(wloc[wn][:, :], w_sl[wn][:, :]).then_inc(*s_cast.inc(16))
                cast_at[wn] = s_cast.v
            g.dma_start(a_bf[:, :], a_in[:, :]).then_inc(*s_cast.inc(16))
            g.dma_start(bpad_bf[:, 0:64], bmat_in[:, :]).then_inc(*s_cast.inc(16))
            g.dma_start(gwpad_bf[0:4, :], gw_in[:, :]).then_inc(*s_cast.inc(16))
            for wn in ("wq", "wk", "wv", "wo", "wf"):
                g.wait_ge(s_cast.h, cast_at[wn])
                g.collective_compute(
                    "AllGather", ALU.bypass, replica_groups=[list(range(NC))],
                    ins=[wloc[wn].opt()], outs=[wbf[wn].opt()]).then_inc(*s_wg.inc(1))
                wg_done[wn] = s_wg.v
        consts_done = s_cast.v

        def ln_stats_and_normalize(v, x_, out_, xx, stat, dv_b, ac_wait):
            v.reduce_sum(stat[:, 0:1], x_[:, :], axis=AX.X).then_inc(*s_dv.inc())
            v.tensor_mul(xx[:, :], x_[:, :], x_[:, :])
            v.reduce_sum(stat[:, 1:2], xx[:, :], axis=AX.X).then_inc(*s_dv.inc())
            ia = v.tensor_scalar(stat[:, 2:3], stat[:, 0:1], 1.0 / D, None, ALU.mult)
            ia._wait_ge(s_dv.h, dv_b + 2)
            ia.then_inc(*s_dv.inc())
            ib = v.tensor_scalar(stat[:, 4:5], stat[:, 1:2], 1.0 / D, None, ALU.mult)
            ib.then_inc(*s_dv.inc())
            i3 = v.tensor_mul(stat[:, 3:4], stat[:, 2:3], stat[:, 2:3])
            i3._wait_ge(s_dv.h, dv_b + 4)
            i3.then_inc(*s_dv.inc())
            i4 = v.tensor_sub(stat[:, 5:6], stat[:, 4:5], stat[:, 3:4])
            i4._wait_ge(s_dv.h, dv_b + 5)
            i4.then_inc(*s_dv.inc())
            v.wait_ge(*ac_wait)
            v.tensor_mul(stat[:, 7:8], stat[:, 2:3], stat[:, 6:7]).then_inc(*s_dv.inc())
            i7 = v.tensor_scalar(stat[:, 8:9], stat[:, 7:8], -1.0, None, ALU.mult)
            i7._wait_ge(s_dv.h, dv_b + 7)
            i7.then_inc(*s_dv.inc())
            ins = v.tensor_scalar(out_[:, :], x_[:, :], stat[:, 6:7], stat[:, 8:9],
                                  ALU.mult, ALU.add)
            ins._wait_ge(s_dv.h, dv_b + 8)
            ins.then_inc(*s_dv.inc())
            assert s_dv.v == dv_b + 9

        def ln_act_rstd(e, stat, dv_wait):
            e.wait_ge(s_dv.h, dv_wait)
            e.activation(stat[:, 9:10], stat[:, 5:6], AF.Ln, bias=EPS).then_inc(*s_ac.inc())
            i2 = e.activation(stat[:, 6:7], stat[:, 9:10], AF.Exp, scale=-0.5)
            i2._wait_ge(s_ac.h, s_ac.v)
            i2.then_inc(*s_ac.inc())

        if phase_limit >= 1:
            pctx = ExitStack()
            sb = lambda n, sh, dt: pctx.enter_context(nc.sbuf_tensor(n, sh, dt))
            xt = [sb(f"p1xt{i}", [128, D], F32) for i in range(2)]
            xx = sb("p1xx", [128, D], F32)
            lt = [sb(f"p1lt{i}", [128, D], BF16) for i in range(2)]
            stat = sb("p1stat", [128, 12], F32)
            x_b, dv_b, ac_b, st_b = s_x.v, s_dv.v, s_ac.v, s_st.v

            @blk.sync
            def _(s):
                for j in range(NT):
                    if P1_DBG == 3:
                        if j >= 1:
                            s.wait_ge(s_dv.h, dv_b + 9 * (j - 1) + 9)
                    elif j >= 2:
                        s.wait_ge(s_dv.h, dv_b + 9 * (j - 2) + 9)
                    s.dma_start(xt[j % 2][:, :], x_sl[j * TT:(j + 1) * TT, :]
                                ).then_inc(*s_x.inc(16))

            @blk.vector
            def _(v):
                for j in range(NT):
                    v.wait_ge(s_x.h, x_b + 16 * (j + 1))
                    if j >= 2:
                        v.wait_ge(s_st.h, st_b + 16 * (j - 1))
                    if P1_DBG == 1 and j >= 1:
                        v.wait_ge(s_ac.h, ac_b + 2 * j)
                    if P1_DBG == 2 and j >= 1:
                        v.wait_ge(s_st.h, st_b + 16 * j)
                    ln_stats_and_normalize(v, xt[j % 2], lt[j % 2], xx, stat,
                                           dv_b + 9 * j, (s_ac.h, ac_b + 2 * j + 2))
                v.tensor_copy(stat[:, 10:11], stat[:, 9:10]).then_inc(*s_dv.inc())

            @blk.scalar
            def _(e):
                for j in range(NT):
                    ln_act_rstd(e, stat, dv_b + 9 * j + 6)

            @blk.gpsimd
            def _(g):
                for j in range(NT):
                    g.wait_ge(s_dv.h, dv_b + 9 * j + 10)
                    g.dma_start(ln1d[j * TT:(j + 1) * TT, :], lt[j % 2][:, :]
                                ).then_inc(*s_st.inc(16))
            pctx.close()

        if phase_limit >= 2:
            pctx = ExitStack()
            sb = lambda n, sh, dt: pctx.enter_context(nc.sbuf_tensor(n, sh, dt))
            lnf = sb("p2lnf", [128, NDC, TOWN], BF16)
            wtile = [sb(f"p2wt{i}", [128, 512], BF16) for i in range(2)]
            brow = {bn: sb(f"{bn}_row", [1, D], F32) for bn in ("bq", "bk", "bv")}
            ps = pctx.enter_context(nc.psum_tensor("p2ps", [128, 4096], F32))
            st_b = s_st.v

            @blk.sync
            def _(s):
                for bn in ("bq", "bk", "bv"):
                    s.dma_start(brow[bn][:, :], b_in[bn][:, :]).then_inc(*s_lf.inc(16))
                s.wait_ge(s_st.h, st_b)
                for dc in range(NDC):
                    s.dma_start(lnf[:, dc, :], ln1d[:, dc * 128:(dc + 1) * 128],
                                transpose=True).then_inc(*s_lf.inc(16))
            lf_done = s_lf.v

            fetch_list = ([("wq", fg, dc) for fg in range(8) for dc in range(NDC)] +
                          [("wk", fg, dc) for fg in range(8) for dc in range(NDC)] +
                          [("wv", fc, dc) for fc in range(8) for dc in range(NDC)])
            wt_b, mmt_b, mm_b, ev_b, st2_b = s_wt.v, s_mmt.v, s_mm.v, s_ev.v, s_st.v

            @blk.sync
            def _(s):
                s.wait_ge(s_wg.h, max(wg_done["wq"], wg_done["wk"], wg_done["wv"]))
                for k, (wn, fg, dc) in enumerate(fetch_list):
                    if k >= 2:
                        s.wait_ge(s_mmt.h, mmt_b + (k - 1))
                    s.dma_start(wtile[k % 2][:, :],
                                wbf[wn][fg * 512:(fg + 1) * 512, dc * 128:(dc + 1) * 128],
                                transpose=True).then_inc(*s_wt.inc(16))

            @blk.tensor
            def _(t):
                t.wait_ge(s_lf.h, lf_done)
                t.wait_ge(s_cast.h, consts_done)
                k = 0
                for wi, wn in enumerate(("wq", "wk")):
                    bn = ("bq", "bk")[wi]
                    for fg in range(8):
                        blki = wi * 8 + fg
                        if blki >= 1:
                            t.wait_ge(s_ev.h, ev_b + 4 * blki)
                        for dc in range(NDC):
                            t.wait_ge(s_wt.h, wt_b + 16 * (k + 1))
                            wt_ = wtile[k % 2]
                            last = None
                            for fi in range(4):
                                for th in range(2):
                                    r0 = (fi * 2 + th) * 512
                                    last = t.matmul(ps[:, r0:r0 + 512],
                                                    wt_[:, fi * 128:(fi + 1) * 128],
                                                    lnf[:, dc, th * 512:(th + 1) * 512],
                                                    start=(dc == 0), stop=False)
                            last.then_inc(*s_mmt.inc())
                            k += 1
                        last = None
                        for fi in range(4):
                            for th in range(2):
                                r0 = (fi * 2 + th) * 512
                                last = t.matmul(ps[:, r0:r0 + 512],
                                                brow[bn][0:1, fg * 512 + fi * 128:
                                                         fg * 512 + (fi + 1) * 128],
                                                ones32r[0:1, 0:512],
                                                start=False, stop=True)
                        last.then_inc(*s_mm.inc())
                for fc in range(8):
                    t.wait_ge(s_ev.h, ev_b + 64 + 8 * fc)
                    for dc in range(NDC):
                        t.wait_ge(s_wt.h, wt_b + 16 * (k + 1))
                        wt_ = wtile[k % 2]
                        last = None
                        for tt in range(8):
                            last = t.matmul(ps[:, tt * 512:(tt + 1) * 512],
                                            lnf[:, dc, tt * 128:(tt + 1) * 128],
                                            wt_[:, :], start=(dc == 0), stop=False)
                        last.then_inc(*s_mmt.inc())
                        k += 1
                    last = None
                    for tt in range(8):
                        last = t.matmul(ps[:, tt * 512:(tt + 1) * 512],
                                        ones32r[0:1, 0:128],
                                        brow["bv"][0:1, fc * 512:(fc + 1) * 512],
                                        start=False, stop=True)
                    last.then_inc(*s_mm.inc())

            @blk.scalar
            def _(e):
                nev = 0
                for wi in range(2):
                    scale = QSCALE if wi == 0 else 1.0
                    for fg in range(8):
                        blki = wi * 8 + fg
                        e.wait_ge(s_mm.h, mm_b + blki + 1)
                        for fi in range(4):
                            if nev >= 2:
                                e.wait_ge(s_st.h, st2_b + 16 * (nev - 1))
                            r0 = (fi * 2) * 512
                            e.activation(stgb[nev % 2][:, 0:512], ps[:, r0:r0 + 512],
                                         AF.Identity, scale=scale)
                            e.activation(stgb[nev % 2][:, 512:1024],
                                         ps[:, r0 + 512:r0 + 1024],
                                         AF.Identity, scale=scale).then_inc(*s_ev.inc())
                            nev += 1
                for fc in range(8):
                    e.wait_ge(s_mm.h, mm_b + 16 + fc + 1)
                    for tt in range(8):
                        if nev >= 2:
                            e.wait_ge(s_st.h, st2_b + 16 * (nev - 1))
                        e.activation(stg16[nev % 2][:, 0:512],
                                     ps[:, tt * 512:(tt + 1) * 512],
                                     AF.Identity).then_inc(*s_ev.inc())
                        nev += 1
                e.activation(stg16[0][:, 1023:1024], stg16[0][:, 1023:1024],
                             AF.Identity).then_inc(*s_ev.inc())

            @blk.gpsimd
            def _(g):
                nev = 0
                for wi in range(2):
                    dst = qf_d if wi == 0 else kf_loc
                    for fg in range(8):
                        for fi in range(4):
                            g.wait_ge(s_ev.h, ev_b + nev + 2)
                            f0 = fg * 512 + fi * 128
                            g.dma_start(dst[f0:f0 + 128, :], stgb[nev % 2][:, :]
                                        ).then_inc(*s_st.inc(16))
                            nev += 1
                g.wait_ge(s_st.h, s_st.v)
                g.collective_compute("AllGather", ALU.bypass,
                                     replica_groups=[list(range(NC))],
                                     ins=[kf_loc.opt()], outs=[kf_g.opt()]
                                     ).then_inc(*s_kv.inc(1))
                for fc in range(8):
                    for tt in range(8):
                        g.wait_ge(s_ev.h, ev_b + nev + 2)
                        g.dma_start(vt_loc[tt * TT:(tt + 1) * TT, fc * 512:(fc + 1) * 512],
                                    stg16[nev % 2][:, 0:512]).then_inc(*s_st.inc(16))
                        nev += 1
                g.wait_ge(s_st.h, s_st.v)
                g.collective_compute("AllGather", ALU.bypass,
                                     replica_groups=[list(range(NC))],
                                     ins=[vt_loc.opt()], outs=[vt_g.opt()]
                                     ).then_inc(*s_kv.inc(1))
            pctx.close()

        if phase_limit >= 3:
            pctx = ExitStack()
            sb = lambda n, sh, dt: pctx.enter_context(nc.sbuf_tensor(n, sh, dt))
            stbuf = [sb("p3st0", [128, 32, 512], FP16), sb("p3st1", [128, 64, 512], FP16)]
            kf_h = [sb(f"p3kf{i}", [128, 64, 128], BF16) for i in range(2)]
            vt_h = [sb(f"p3vt{i}", [128, 64, 128], FP16) for i in range(2)]
            qf_h = [sb(f"p3qf{i}", [128, 1024], BF16) for i in range(2)]
            oscr = sb("p3oscr", [128, 8, 512], FP16)
            maxt = sb("p3maxt", [128, 512], FP16)
            mbc = sb("p3mbc", [128, 512], FP16)
            rbc = sb("p3rbc", [128, 512], F32)
            mq4 = sb("p3mq4", [128, 4], F32)
            rrow = sb("p3rrow", [1, 512], F32)
            ediag4 = sb("p3ed", [128, 4, 128], FP16)
            avst = [sb(f"p3av{i}", [128, 512], BF16) for i in range(2)]
            st_ps = pctx.enter_context(nc.psum_tensor("p3stps", [128, 2048], F32))
            av_ps = pctx.enter_context(nc.psum_tensor("p3avps", [128, 512], F32))
            l_ps = pctx.enter_context(nc.psum_tensor("p3lps", [1, 512], F32))
            mb_ps = pctx.enter_context(nc.psum_tensor("p3mbps", [128, 512], F32))
            pst_ps = pctx.enter_context(nc.psum_tensor("p3pst", [128, 512], FP16))

            units = _p3_units(n_heads)
            NU = len(units)
            kv_need = s_kv.v
            ad_b = s_ad.v
            M = _p3_marks(n_heads, s_ap.v, s_ae.v, s_at.v, s_st.v)

            @blk.sync
            def _(s):
                s.wait_ge(s_kv.h, kv_need - 1)
                for h in range(n_heads):
                    if h >= 2:
                        s.wait_ge(s_ap.h, M[("avg", 2 * (h - 2) + 1)])
                    hb = h % 2
                    for r in range(NC):
                        src = kf_g[r * D + h * 128: r * D + (h + 1) * 128, :]
                        src = src.rearrange("p (j k) -> p j k", k=128)
                        dst = kf_h[hb].rearrange("p (j r) k -> p j r k", r=8)[:, :, r, :]
                        s.dma_start(dst, src).then_inc(*s_ad.inc(16))
                    s.dma_start(qf_h[hb][:, :], qf_d[h * 128:(h + 1) * 128, :]
                                ).then_inc(*s_ad.inc(16))
                    if h == 0:
                        s.wait_ge(s_kv.h, kv_need)
                    for r in range(NC):
                        src = vt_g[r * 1024:(r + 1) * 1024, h * 128:(h + 1) * 128]
                        src = src.rearrange("(j p) k -> p j k", p=128)
                        dst = vt_h[hb].rearrange("p (j r) k -> p j r k", r=8)[:, :, r, :]
                        s.dma_start(dst, src).then_inc(*s_ad.inc(16))

            gkb0 = {}
            _g = 0
            for u in range(NU):
                gkb0[u] = _g
                _g += _nkb(units[u][1])

            @blk.tensor
            def _(t):
                for e in _p3_pe_events(n_heads):
                    if e[0] == "kb":
                        _, u, kb = e
                        h, half = units[u]
                        hb = h % 2
                        if kb == 0:
                            t.wait_ge(s_ad.h, ad_b + 16 * 17 * (h + 1))
                            t.wait_ge(s_kv.h, kv_need - 1)
                            if u >= 2:
                                t.wait_ge(s_ap.h, M[("avg", u - 2)])
                        g = gkb0[u] + kb
                        off = _off(half, kb)
                        slot = g % 4
                        if g >= 4:
                            t.wait_ge(s_ae.h, M["evacv"][g - 4])
                        t.matmul(st_ps[:, slot * 512 + off:(slot + 1) * 512],
                                 kf_h[hb][:, kb, :],
                                 qf_h[hb][:, half * 512 + off:(half + 1) * 512],
                                 start=True, stop=True).then_inc(*s_ap.inc())
                        assert s_ap.v == M[("kbv", u, kb)]
                    elif e[0] == "T":
                        u = e[1]
                        t.wait_ge(s_at.h, M[("max", u)])
                        last = None
                        for i in range(4):
                            last = t.transpose(pst_ps[:, i * 128:(i + 1) * 128],
                                               maxt[:, i * 128:(i + 1) * 128],
                                               ident16[:, :])
                        last.then_inc(*s_ap.inc())
                        assert s_ap.v == M[("tposed", u)]
                    elif e[0] == "mb":
                        u = e[1]
                        t.wait_ge(s_at.h, M[("ediag", u)])
                        last = None
                        for i in range(4):
                            last = t.matmul(mb_ps[:, i * 128:(i + 1) * 128],
                                            ones16[:, :], ediag4[:, i, :],
                                            start=True, stop=True)
                        last.then_inc(*s_ap.inc())
                        assert s_ap.v == M[("mbmm", u)]
                    elif e[0] == "AV":
                        u = e[1]
                        h, half = units[u]
                        hb = h % 2
                        nkb = _nkb(half)
                        sbuf_ = stbuf[half]
                        t.wait_ge(s_ae.h, M[("exp", u)])
                        t.wait_ge(s_kv.h, kv_need)
                        if u >= 1:
                            t.wait_ge(s_at.h, M[("avscaled", u - 1)])
                        if u >= 2:
                            t.wait_ge(s_st.h, M[("avstored", u - 2)])
                        last = None
                        for kb in range(nkb):
                            off = _off(half, kb)
                            t.matmul(av_ps[:, off:512], vt_h[hb][:, kb, :],
                                     sbuf_[:, kb, off:512], start=(kb == 0),
                                     stop=(kb == nkb - 1), skip_group_check=True)
                            last = t.matmul(l_ps[0:1, off:512], ones16[:, 0:1],
                                            sbuf_[:, kb, off:512], start=(kb == 0),
                                            stop=(kb == nkb - 1), skip_group_check=True)
                        last.then_inc(*s_ap.inc())
                        assert s_ap.v == M[("avg", u)]
                    elif e[0] == "rbc":
                        u = e[1]
                        t.wait_ge(s_at.h, M[("recip", u)])
                        t.matmul(mb_ps[:, 0:512], ones32r[0:1, 0:128], rrow[0:1, :],
                                 start=True, stop=True).then_inc(*s_ap.inc())
                        assert s_ap.v == M[("rbcmm", u)]
                assert s_ap.v == M["ap_end"]

            @blk.scalar
            def _(e):
                for ev in _p3_act_events(n_heads):
                    kind, u = ev
                    h, half = units[u]
                    nkb = _nkb(half)
                    sbuf_ = stbuf[half]
                    if kind == "evacs":
                        if u >= 2:
                            e.wait_ge(s_ap.h, M[("avg", u - 2)])
                        for kb in range(nkb):
                            off = _off(half, kb)
                            slot = (gkb0[u] + kb) % 4
                            e.wait_ge(s_ap.h, M[("kbv", u, kb)])
                            e.activation(sbuf_[:, kb, off:512],
                                         st_ps[:, slot * 512 + off:(slot + 1) * 512],
                                         AF.Identity).then_inc(*s_ae.inc())
                        assert s_ae.v == M[("evac", u)]
                    elif kind == "mbc":
                        e.wait_ge(s_ap.h, M[("mbmm", u)])
                        e.activation(mbc[:, :], mb_ps[:, :], AF.Identity
                                     ).then_inc(*s_ae.inc())
                        assert s_ae.v == M[("mbc", u)]
                    elif kind == "exp":
                        e.wait_ge(s_at.h, M[("sub", u)])
                        flat = sbuf_.rearrange("p a b -> p (a b)")
                        e.activation(flat, flat, AF.Exp).then_inc(*s_ae.inc())
                        assert s_ae.v == M[("exp", u)]

            av_shield = {}
            @blk.vector
            def _(v):
                v.wait_ge(s_cast.h, consts_done)
                for ev in _p3_dve_events(n_heads):
                    kind, u = ev
                    h, half = units[u]
                    nkb = _nkb(half)
                    no = nkb // 8
                    sbuf_ = stbuf[half]
                    if kind == "oct":
                        if u >= 1:
                            v.wait_ge(s_ap.h, M[("tposed", u - 1)])
                        for o in range(no):
                            off = _off(half, 8 * o)
                            v.wait_ge(s_ae.h, M[("evbase", u)] + 8 * (o + 1))
                            mask_edge = None
                            if _masked_octet(half, o):
                                sl = sbuf_[:, 8 * o:8 * o + 8, off:off + 128]
                                v.tensor_add(sl, sl, mkey_s[:, :, :]).then_inc(*s_am.inc())
                                mask_edge = s_am.v
                            if off > 0:
                                v.memset(oscr[:, o, 0:off], -60000.0)
                            red = v.reduce_max(
                                oscr[:, o, off:512],
                                sbuf_[:, 8 * o:8 * o + 8, off:512].rearrange("p a b -> p b a"),
                                axis=AX.X)
                            if mask_edge is not None:
                                red._wait_ge(s_am.h, mask_edge)
                            if o == no - 1:
                                red.then_inc(*s_at.inc())
                        assert s_at.v == M[("oct", u)]
                        fin = v.reduce_max(maxt[:, :],
                                           oscr[:, 0:no, :].rearrange("p a b -> p b a"),
                                           axis=AX.X)
                        fin._wait_ge(s_at.h, M[("oct", u)])
                        fin.then_inc(*s_at.inc())
                        assert s_at.v == M[("max", u)]
                    elif kind == "mq":
                        v.wait_ge(s_ap.h, M[("tposed", u)])
                        last = None
                        for i in range(4):
                            last = v.reduce_max(mq4[:, i:i + 1],
                                                pst_ps[:, i * 128:(i + 1) * 128], axis=AX.X)
                        last.then_inc(*s_at.inc())
                        assert s_at.v == M[("redge", u)]
                        last = None
                        for i in range(4):
                            ins = v.tensor_scalar(ediag4[:, i, :], ident16[:, :],
                                                  mq4[:, i:i + 1], None, ALU.mult)
                            ins._wait_ge(s_at.h, M[("redge", u)])
                            last = ins
                        last.then_inc(*s_at.inc())
                        assert s_at.v == M[("ediag", u)]
                    elif kind == "sub":
                        v.wait_ge(s_ae.h, M[("mbc", u)])
                        bview = mbc.rearrange("p (o q) -> p o q", o=1).broadcast_to(
                            [128, nkb, 512])
                        v.tensor_sub(sbuf_[:, :, :], sbuf_[:, :, :], bview
                                     ).then_inc(*s_at.inc())
                        assert s_at.v == M[("sub", u)]
                    elif kind == "fin":
                        v.wait_ge(s_ap.h, M[("avg", u)])
                        v.reciprocal(rrow[0:1, :], l_ps[0:1, :]).then_inc(*s_at.inc())
                        assert s_at.v == M[("recip", u)]
                        v.wait_ge(s_ap.h, M[("rbcmm", u)])
                        v.tensor_copy(rbc[:, :], mb_ps[:, :]).then_inc(*s_at.inc())
                        assert s_at.v == M[("rbccp", u)]
                        if u >= 2:
                            v.wait_ge(s_st.h, M[("avstored", u - 2)])
                        im2 = v.tensor_mul(avst[u % 2][:, :], av_ps[:, :], rbc[:, :])
                        im2._wait_ge(s_at.h, M[("rbccp", u)])
                        im2.then_inc(*s_at.inc())
                        assert s_at.v == M[("avscaled", u)]
                        v.tensor_copy(mq4[:, 0:1], mq4[:, 1:2]).then_inc(*s_am.inc())
                        av_shield[u] = s_am.v

            @blk.gpsimd
            def _(g):
                for u in range(NU):
                    h, half = units[u]
                    g.wait_ge(s_am.h, av_shield[u])
                    g.dma_start(avf_d[h * 128:(h + 1) * 128, half * 512:(half + 1) * 512],
                                avst[u % 2][:, :]).then_inc(*s_st.inc(16))
                    assert s_st.v == M[("avstored", u)]
            pctx.close()

        def proj_phase(tag, wsrc, bias_src, lhs_loader, lora, dst, wg_need):
            pctx = ExitStack()
            sb = lambda n, sh, dt: pctx.enter_context(nc.sbuf_tensor(n, sh, dt))
            w8 = [sb(f"{tag}w{i}", [128, NDC, 512], BF16) for i in range(2)]
            bias_row = sb(f"{tag}brow", [1, D], F32)
            ps4 = pctx.enter_context(nc.psum_tensor(f"{tag}ps", [128, 2048], F32))
            lhs = lhs_loader(pctx, sb)

            wt_b, mm_b, ev_b, st_b = s_wt.v, s_mm.v, s_ev.v, s_st.v
            @blk.sync
            def _(s):
                s.dma_start(bias_row[:, :], bias_src[:, :]).then_inc(*s_x.inc(16))
            bias_ld = s_x.v

            @blk.sync
            def _(s):
                s.wait_ge(s_wg.h, wg_need)
                for o8 in range(8):
                    if o8 >= 2:
                        s.wait_ge(s_mm.h, mm_b + 8 * (o8 - 1))
                    for dc in range(NDC):
                        s.dma_start(w8[o8 % 2][:, dc, :],
                                    wsrc[o8 * 512:(o8 + 1) * 512, dc * 128:(dc + 1) * 128],
                                    transpose=True).then_inc(*s_wt.inc(16))

            @blk.tensor
            def _(t):
                lhs.pe_wait(t)
                t.wait_ge(s_x.h, bias_ld)
                for o8 in range(8):
                    t.wait_ge(s_wt.h, wt_b + 16 * 32 * (o8 + 1))
                    for tt in range(8):
                        grp = o8 * 8 + tt
                        slot = grp % 4
                        if grp >= 4:
                            t.wait_ge(s_ev.h, ev_b + (grp - 3))
                        for dc in range(NDC):
                            t.matmul(ps4[:, slot * 512:(slot + 1) * 512],
                                     lhs.get(dc, tt), w8[o8 % 2][:, dc, :],
                                     start=(dc == 0), stop=False)
                        if lora is not None:
                            umf_s, btT_s = lora
                            t.matmul(ps4[:, slot * 512:(slot + 1) * 512],
                                     umf_s[0:64, tt * 128:(tt + 1) * 128],
                                     btT_s[0:64, o8 * 512:(o8 + 1) * 512],
                                     start=False, stop=False)
                        t.matmul(ps4[:, slot * 512:(slot + 1) * 512],
                                 ones32r[0:1, 0:128],
                                 bias_row[0:1, o8 * 512:(o8 + 1) * 512],
                                 start=False, stop=True).then_inc(*s_mm.inc())

            @blk.scalar
            def _(e):
                for grp in range(64):
                    slot = grp % 4
                    e.wait_ge(s_mm.h, mm_b + grp + 1)
                    if grp >= 2:
                        e.wait_ge(s_st.h, st_b + 16 * (grp - 1))
                    e.activation(stgb[grp % 2][:, 0:512],
                                 ps4[:, slot * 512:(slot + 1) * 512],
                                 AF.Identity).then_inc(*s_ev.inc())
                e.activation(stgb[0][:, 1023:1024], stgb[0][:, 1023:1024],
                             AF.Identity).then_inc(*s_ev.inc())

            @blk.gpsimd
            def _(g):
                for grp in range(64):
                    o8, tt = grp // 8, grp % 8
                    g.wait_ge(s_ev.h, ev_b + grp + 2)
                    g.dma_start(dst[tt * TT:(tt + 1) * TT, o8 * 512:(o8 + 1) * 512],
                                stgb[grp % 2][:, 0:512]).then_inc(*s_st.inc(16))
            pctx.close()

        if phase_limit >= 4:
            class _AvfLhs:
                def alloc(self, pctx, sb):
                    self.buf = sb("p4avf", [128, NDC, 8, 128], BF16)
                    st_need = s_st.v
                    @blk.sync
                    def _(s):
                        s.wait_ge(s_st.h, st_need)
                        s.dma_start(self.buf[:, :, :, :],
                                    avf_d.rearrange("(c p) (t q) -> p c t q", p=128, q=128)
                                    ).then_inc(*s_x.inc(16))
                    self.need = s_x.v
                    return self

                def pe_wait(self, t):
                    t.wait_ge(s_x.h, self.need)

                def get(self, dc, tt):
                    return self.buf[:, dc, tt, :]

            lhs4 = _AvfLhs()
            proj_phase("p4", wbf["wo"], b_in["bo"],
                       lambda p, s: lhs4.alloc(p, s), None, y1_d, wg_done["wo"])

        if phase_limit >= 5:
            pctx = ExitStack()
            sb = lambda n, sh, dt: pctx.enter_context(nc.sbuf_tensor(n, sh, dt))
            xt = [sb(f"p5xt{i}", [128, D], F32) for i in range(2)]
            y1t = [sb(f"p5y1{i}", [128, D], BF16) for i in range(2)]
            zz = sb("p5zz", [128, D], F32)
            xx = sb("p5xx", [128, D], F32)
            lt = [sb(f"p5lt{i}", [128, D], BF16) for i in range(2)]
            stat = sb("p5stat", [128, 12], F32)
            x_b, dv_b, ac_b, st_b = s_x.v, s_dv.v, s_ac.v, s_st.v
            st_need = s_st.v
            DVT = 20

            @blk.sync
            def _(s):
                s.wait_ge(s_st.h, st_need)
                for j in range(NT):
                    if j >= 2:
                        s.wait_ge(s_dv.h, dv_b + DVT * (j - 2) + DVT)
                        s.wait_ge(s_st.h, st_b + 32 * (j - 1))
                    s.dma_start(y1t[j % 2][:, :], y1_d[j * TT:(j + 1) * TT, :]
                                ).then_inc(*s_x.inc(16))
                    s.dma_start(xt[j % 2][:, :], x_sl[j * TT:(j + 1) * TT, :]
                                ).then_inc(*s_x.inc(16))

            @blk.vector
            def _(v):
                for j in range(NT):
                    y_, x_ = y1t[j % 2], xt[j % 2]
                    v.wait_ge(s_x.h, x_b + 32 * (j + 1))
                    base = dv_b + DVT * j
                    ln_stats_and_normalize(v, y_, zz, xx, stat, base,
                                           (s_ac.h, ac_b + 4 * j + 2))
                    ia = v.tensor_add(x_[:, :], zz[:, :], x_[:, :])
                    ia._wait_ge(s_dv.h, base + 9)
                    ia.then_inc(*s_dv.inc())
                    v.wait_ge(s_dv.h, base + 10)
                    ln_stats_and_normalize(v, x_, lt[j % 2], xx, stat, base + 10,
                                           (s_ac.h, ac_b + 4 * j + 4))
                    v.tensor_copy(stat[:, 10:11], stat[:, 11:12]).then_inc(*s_dv.inc())
                    assert s_dv.v == base + DVT

            @blk.scalar
            def _(e):
                for j in range(NT):
                    base = dv_b + DVT * j
                    ln_act_rstd(e, stat, base + 6)
                    ln_act_rstd(e, stat, base + 10 + 6)

            @blk.gpsimd
            def _(g):
                for j in range(NT):
                    base = dv_b + DVT * j
                    g.wait_ge(s_dv.h, base + 11)
                    g.dma_start(x1_d[j * TT:(j + 1) * TT, :], xt[j % 2][:, :]
                                ).then_inc(*s_st.inc(16))
                    g.wait_ge(s_dv.h, base + 20)
                    g.dma_start(ln3d[j * TT:(j + 1) * TT, :], lt[j % 2][:, :]
                                ).then_inc(*s_st.inc(16))
            pctx.close()

        if phase_limit >= 6:
            pctx6 = ExitStack()
            sb6 = lambda n, sh, dt: pctx6.enter_context(nc.sbuf_tensor(n, sh, dt))
            lnf6 = sb6("p6lnf", [128, NDC, TOWN], BF16)
            at_s = sb6("p6at", [128, NDC, 64], BF16)
            gwT_s = sb6("p6gw", [128, NDC, 16], BF16)
            btT_s = sb6("p6bt", [128, D], BF16)
            umf_s = sb6("p6umf", [64, 1024], BF16)
            maskf_s = sb6("p6mask", [64, 1024], BF16)
            scores_s = sb6("p6sc", [128, 32], F32)
            oh_s = sb6("p6oh", [128, 32], F32)
            ecol_s = sb6("p6ec", [128, 8], F32)
            gst = sb6("p6gst", [128, 8], F32)
            edb = sb6("p6edb", [128, 8, 128], BF16)
            gb_row = sb6("p6gb", [1, 4], F32)
            psg = pctx6.enter_context(nc.psum_tensor("p6psg", [128, 512], F32))
            psu = pctx6.enter_context(nc.psum_tensor("p6psu", [64, 1024], F32))
            pse = pctx6.enter_context(nc.psum_tensor("p6pse", [64, 128], F32))

            st_need6 = s_st.v
            @blk.sync
            def _(s):
                s.wait_ge(s_st.h, st_need6)
                for dc in range(NDC):
                    s.dma_start(lnf6[:, dc, :], ln3d[:, dc * 128:(dc + 1) * 128],
                                transpose=True).then_inc(*s_lf.inc(16))
                s.wait_ge(s_cast.h, consts_done)
                for dc in range(NDC):
                    s.dma_start(at_s[:, dc, :], a_bf[:, dc * 128:(dc + 1) * 128],
                                transpose=True).then_inc(*s_lf.inc(16))
                for dc in range(NDC):
                    s.dma_start(gwT_s[:, dc, :], gwpad_bf[:, dc * 128:(dc + 1) * 128],
                                transpose=True).then_inc(*s_lf.inc(16))
                s.dma_start(btT_s[:, :], bpad_bf[:, :], transpose=True
                            ).then_inc(*s_lf.inc(16))
                s.dma_start(gb_row[:, :], gb_in[:, :]).then_inc(*s_lf.inc(16))
            lf6_done = s_lf.v

            mm6_b, dv6_b = s_mm.v, s_dv.v
            @blk.tensor
            def _(t):
                t.wait_ge(s_lf.h, lf6_done)
                last = None
                for tt in range(8):
                    for dc in range(NDC):
                        t.matmul(psg[:, tt * 4:(tt + 1) * 4],
                                 lnf6[:, dc, tt * 128:(tt + 1) * 128],
                                 gwT_s[:, dc, 0:4], start=(dc == 0), stop=False)
                    last = t.matmul(psg[:, tt * 4:(tt + 1) * 4], ones32r[0:1, 0:128],
                                    gb_row[0:1, 0:4], start=False, stop=True)
                last.then_inc(*s_mm.inc())
                last = None
                for th in range(2):
                    for dc in range(NDC):
                        last = t.matmul(psu[0:64, th * 512:(th + 1) * 512],
                                        at_s[:, dc, :],
                                        lnf6[:, dc, th * 512:(th + 1) * 512],
                                        start=(dc == 0), stop=(dc == NDC - 1))
                last.then_inc(*s_mm.inc())
                for tt in range(8):
                    t.wait_ge(s_dv.h, dv6_b + 5 + tt + 1)
                    if tt >= 1:
                        t.wait_ge(s_dv.h, dv6_b + 13 + tt)
                    t.matmul(pse[0:64, :], onesbf[:, 0:64], edb[:, tt, :],
                             start=True, stop=True).then_inc(*s_mm.inc())

            @blk.vector
            def _(v):
                v.wait_ge(s_mm.h, mm6_b + 1)
                v.tensor_copy(scores_s[:, :], psg[:, 0:32]).then_inc(*s_dv.inc())
                last = None
                for tt in range(8):
                    ins = v.reduce_max(gst[:, tt:tt + 1], scores_s[:, tt * 4:(tt + 1) * 4],
                                       axis=AX.X)
                    ins._wait_ge(s_dv.h, dv6_b + 1)
                    last = ins
                last.then_inc(*s_dv.inc())
                last = None
                for tt in range(8):
                    ins = v.tensor_scalar(oh_s[:, tt * 4:(tt + 1) * 4],
                                          scores_s[:, tt * 4:(tt + 1) * 4],
                                          gst[:, tt:tt + 1], None, ALU.is_ge)
                    ins._wait_ge(s_dv.h, dv6_b + 2)
                    last = ins
                last.then_inc(*s_dv.inc())
                oh3 = oh_s.rearrange("p (a b) -> p a b", b=4)
                ins = v.tensor_mul(oh3, oh3,
                                   iota4_s.rearrange("p (o q) -> p o q", o=1)
                                   .broadcast_to([128, 8, 4]))
                ins._wait_ge(s_dv.h, dv6_b + 3)
                ins.then_inc(*s_dv.inc())
                last = None
                for tt in range(8):
                    ins = v.reduce_sum(ecol_s[:, tt:tt + 1], oh_s[:, tt * 4:(tt + 1) * 4],
                                       axis=AX.X)
                    ins._wait_ge(s_dv.h, dv6_b + 4)
                    last = ins
                last.then_inc(*s_dv.inc())
                for tt in range(8):
                    ins = v.tensor_scalar(edb[:, tt, :], identbf[:, :],
                                          ecol_s[:, tt:tt + 1], None, ALU.mult)
                    ins._wait_ge(s_dv.h, dv6_b + 5)
                    ins.then_inc(*s_dv.inc())
                for tt in range(8):
                    v.wait_ge(s_mm.h, mm6_b + 2 + tt + 1)
                    v.tensor_scalar(maskf_s[0:64, tt * 128:(tt + 1) * 128],
                                    pse[0:64, :], lvl_s[0:64, 0:1], None, ALU.is_ge
                                    ).then_inc(*s_dv.inc())
                v.wait_ge(s_mm.h, mm6_b + 2)
                ins = v.tensor_mul(umf_s[0:64, :], psu[0:64, :], maskf_s[0:64, :])
                ins._wait_ge(s_dv.h, dv6_b + 21)
                ins.then_inc(*s_dv.inc())
            umf_done = s_dv.v

            class _LnfLhs:
                def alloc(self, pctx, sb):
                    return self

                def pe_wait(self, t):
                    t.wait_ge(s_lf.h, lf6_done)
                    t.wait_ge(s_dv.h, umf_done)

                def get(self, dc, tt):
                    return lnf6[:, dc, tt * 128:(tt + 1) * 128]

            proj_phase("p6w", wbf["wf"], b_in["bf"], lambda p, s: _LnfLhs().alloc(p, s),
                       (umf_s, btT_s), y2_d, wg_done["wf"])
            pctx6.close()

        if phase_limit >= 7:
            pctx = ExitStack()
            sb = lambda n, sh, dt: pctx.enter_context(nc.sbuf_tensor(n, sh, dt))
            x1t = [sb(f"p7x{i}", [128, D], F32) for i in range(2)]
            y2t = [sb(f"p7y{i}", [128, D], BF16) for i in range(2)]
            zz = sb("p7zz", [128, D], F32)
            xx = sb("p7xx", [128, D], F32)
            stat = sb("p7stat", [128, 12], F32)
            x_b, dv_b, ac_b, st_b = s_x.v, s_dv.v, s_ac.v, s_st.v
            st_need = s_st.v
            DVT = 10

            @blk.sync
            def _(s):
                s.wait_ge(s_st.h, st_need)
                for j in range(NT):
                    if j >= 2:
                        s.wait_ge(s_dv.h, dv_b + DVT * (j - 2) + DVT)
                        s.wait_ge(s_st.h, st_b + 16 * (j - 1))
                    s.dma_start(y2t[j % 2][:, :], y2_d[j * TT:(j + 1) * TT, :]
                                ).then_inc(*s_x.inc(16))
                    s.dma_start(x1t[j % 2][:, :], x1_d[j * TT:(j + 1) * TT, :]
                                ).then_inc(*s_x.inc(16))

            @blk.vector
            def _(v):
                for j in range(NT):
                    y_, x_ = y2t[j % 2], x1t[j % 2]
                    v.wait_ge(s_x.h, x_b + 32 * (j + 1))
                    base = dv_b + DVT * j
                    ln_stats_and_normalize(v, y_, zz, xx, stat, base,
                                           (s_ac.h, ac_b + 2 * j + 2))
                    ia = v.tensor_add(x_[:, :], zz[:, :], x_[:, :])
                    ia._wait_ge(s_dv.h, base + 9)
                    ia.then_inc(*s_dv.inc())
                    assert s_dv.v == base + DVT
                v.tensor_copy(stat[:, 10:11], stat[:, 9:10]).then_inc(*s_dv.inc())

            @blk.scalar
            def _(e):
                for j in range(NT):
                    ln_act_rstd(e, stat, dv_b + DVT * j + 6)

            @blk.gpsimd
            def _(g):
                for j in range(NT):
                    g.wait_ge(s_dv.h, dv_b + DVT * j + 11)
                    g.dma_start(y_out[j * TT:(j + 1) * TT, :], x1t[j % 2][:, :]
                                ).then_inc(*s_st.inc(16))
            pctx.close()

        if phase_limit == 2 and "kf_g2" in taps:
            kf_g2 = nc.dram_tensor("kf_g2", [NC * D, TOWN], BF16).ap()
            dram_map["kf_g2"] = kf_g2
            @blk.gpsimd
            def _(g):
                g.wait_ge(s_kv.h, s_kv.v)
                g.collective_compute("AllGather", ALU.bypass,
                                     replica_groups=[list(range(NC))],
                                     ins=[kf_loc.opt()], outs=[kf_g2.opt()]
                                     ).then_inc(*s_kv.inc(1))
                g.wait_ge(s_kv.h, s_kv.v)

        @blk.gpsimd
        def _(g):
            g.wait_ge(s_st.h, s_st.v)
            g.wait_ge(s_kv.h, s_kv.v)
            for name in taps:
                src = dram_map[name]
                out = nc.dram_tensor(f"dbg_{name}", list(src.shape), src.dtype,
                                     kind="ExternalOutput").ap()
                dbg_names[name] = f"dbg_{name}"
                g.dma_start(out, src).then_inc(*s_st.inc(16))
            g.wait_ge(s_st.h, s_st.v)

    return nc, dbg_names


def _mkey_for_core(c):
    m = np.full((NC * TT, TT), NEGBIG, dtype=np.float32)
    for r in range(NC):
        if r < c:
            m[r * TT:(r + 1) * TT, :] = 0.0
        elif r == c:
            kk = np.arange(TT)[:, None]
            qq = np.arange(TT)[None, :]
            m[r * TT:(r + 1) * TT, :] = np.where(kk <= qq, 0.0, NEGBIG)
    return m


def make_in_maps(inputs):
    f32 = lambda a: np.ascontiguousarray(np.asarray(a, dtype=np.float32))
    x = f32(inputs["L0_attn_x"]).reshape(T, D)
    W = {"wq": f32(inputs["L0_attn_Wq"]), "wk": f32(inputs["L0_attn_Wk"]),
         "wv": f32(inputs["L0_attn_Wv"]), "wo": f32(inputs["L0_attn_Wo"]),
         "wf": f32(inputs["L0_ffn_W"])}
    B = {"bq": f32(inputs["L0_attn_bq"]), "bk": f32(inputs["L0_attn_bk"]),
         "bv": f32(inputs["L0_attn_bv"]), "bo": f32(inputs["L0_attn_bo"]),
         "bf": f32(inputs["L0_ffn_b"])}
    A = f32(inputs["L0_ffn_A"])
    Bm = f32(inputs["L0_ffn_B"])
    gw = f32(inputs["L0_ffn_gate_w"])
    gb = f32(inputs["L0_ffn_gate_b"])
    xr = x.reshape(T // TT, TT, D)
    in_maps = []
    for c in range(NC):
        m = {"x_sl": np.ascontiguousarray(xr[c::NC].reshape(TOWN, D)),
             "a_in": A, "bmat_in": Bm, "gw_in": gw,
             "gb_in": gb.reshape(1, 4),
             "mkey_in": _mkey_for_core(c)}
        for wn in W:
            m[f"{wn}_sl"] = np.ascontiguousarray(W[wn][c * (D // NC):(c + 1) * (D // NC)])
        for bn in B:
            m[f"{bn}_in"] = B[bn].reshape(1, D)
        in_maps.append(m)
    return in_maps


def assemble_output(results):
    y = np.empty((T, D), dtype=np.float32)
    yr = y.reshape(T // TT, TT, D)
    for c in range(NC):
        yr[c::NC] = results[c]["y_out"].reshape(NT, TT, D)
    return y.reshape(1, T, D)


def kernel(**inputs):
    nc, _ = build_program()
    in_maps = make_in_maps(inputs)
    res = run_bass_kernel_spmd(nc, in_maps, core_ids=list(range(NC)))
    return assemble_output(res.results)
